# version 02
# baseline (speedup 1.0000x reference)
import math
import numpy as np
import concourse.bass as bass
import concourse.mybir as mybir
from concourse.bass_utils import run_bass_kernel_spmd

F32 = mybir.dt.float32
BF16 = mybir.dt.bfloat16
I32 = mybir.dt.int32
AF = mybir.ActivationFunctionType
ALU = mybir.AluOpType
AX = mybir.AxisListType

PSUM_BANK_BYTES = 2048


ENGINES = ("tensor", "vector", "scalar", "pool", "sp")
DMA_QUEUES = {"sp": 0, "scalar": 1, "pool": 2}


class Tile:

    __slots__ = ("t", "pool", "lo", "hi", "part0", "parts", "coff", "cols", "dtype")

    def __init__(self, t, pool, lo, hi, part0, parts, coff, cols, dtype):
        self.t = t
        self.pool = pool
        self.lo = lo
        self.hi = hi
        self.part0 = part0
        self.parts = parts
        self.coff = coff
        self.cols = cols
        self.dtype = dtype

    def ap(self):
        try:
            same = self.t.dtype == self.dtype
        except AttributeError:
            same = True
        if same:
            return self.t[self.part0:self.part0 + self.parts,
                          self.coff:self.coff + self.cols]
        return self.t[self.part0:self.part0 + self.parts, :].bitcast(self.dtype)[
            :, self.coff:self.coff + self.cols]

    def cview(self, c0, n):
        assert 0 <= c0 and c0 + n <= self.cols, (c0, n, self.cols)
        if self.pool == "psum":
            return Tile(self.t, self.pool, self.lo, self.hi,
                        self.part0, self.parts, self.coff + c0, n, self.dtype)
        esz = _esize(self.dtype)
        return Tile(self.t, self.pool, self.lo + c0 * esz, self.lo + (c0 + n) * esz,
                    self.part0, self.parts, self.coff + c0, n, self.dtype)

    def pview(self, p0, np_):
        assert 0 <= p0 and p0 + np_ <= self.parts
        return Tile(self.t, self.pool, self.lo, self.hi,
                    self.part0 + p0, np_, self.coff, self.cols, self.dtype)

    @property
    def shape(self):
        return (self.parts, self.cols)


def _esize(dt):
    return {F32: 4, BF16: 2, I32: 4, mybir.dt.float16: 2, mybir.dt.uint8: 1}[dt]


class DramT:

    __slots__ = ("t", "pool", "shape", "dtype", "esz")

    def __init__(self, t, pool, shape, dtype):
        self.t = t
        self.pool = pool
        self.shape = shape
        self.dtype = dtype
        self.esz = _esize(dtype)

    def range_for(self, index_expr_lo, index_expr_hi):
        return (index_expr_lo * self.esz, index_expr_hi * self.esz)


class _Op:
    __slots__ = ("eng", "emit", "waits", "inc", "dma_q", "dma_slot", "tag", "seq")

    def __init__(self, eng, emit, waits, dma_q, tag, seq):
        self.eng = eng
        self.emit = emit
        self.waits = waits
        self.inc = False
        self.dma_q = dma_q
        self.dma_slot = None
        self.tag = tag
        self.seq = seq


DMA_RING = 24


class TK:
    def __init__(self, nc):
        self.nc = nc
        self.ops = []
        self.pools = {}
        self.dma_issue = {q: 0 for q in DMA_QUEUES.values()}
        self.dma_use = {}
        self.wm = {e: {} for e in ENGINES}
        self._eng_seq = {e: 0 for e in ENGINES}
        self._op_by_ev = {}
        self.frozen = False

    def _resolve(self, eng, reads, writes):
        need = {}

        def add_ev(ev):
            if ev is None:
                return
            kind = ev[0]
            if kind == "e":
                _, peng, pseq = ev
                if peng == eng == "tensor":
                    return
                if peng == eng and pseq is not None:
                    pass
                key = ("e", peng)
                cur = need.get(key, -1)
                if pseq > cur:
                    need[key] = pseq
            else:
                _, q, slot, n = ev
                key = ("d", q, slot)
                cur = need.get(key, -1)
                if n > cur:
                    need[key] = n

        for (pool, lo, hi) in reads:
            for seg in self._overlap(pool, lo, hi):
                add_ev(seg[2])
        for (pool, lo, hi) in writes:
            for seg in self._overlap(pool, lo, hi):
                add_ev(seg[2])
                for rev in seg[3]:
                    add_ev(rev)

        waits = []
        wmk = self.wm[eng]
        for key, val in need.items():
            if key[0] == "e" and key[1] == eng:
                pass
            if wmk.get(key, -1) >= val:
                continue
            wmk[key] = val
            waits.append((key, val))
        return waits

    def _overlap(self, pool, lo, hi):
        segs = self.pools.get(pool)
        if not segs:
            return ()
        return [s for s in segs if s[0] < hi and s[1] > lo]

    def _commit_write(self, pool, lo, hi, ev):
        segs = self.pools.setdefault(pool, [])
        out = []
        for s in segs:
            if s[0] < hi and s[1] > lo:
                if s[0] < lo:
                    out.append([s[0], lo, s[2], s[3]])
                if s[1] > hi:
                    out.append([hi, s[1], s[2], s[3]])
            else:
                out.append(s)
        out.append([lo, hi, ev, []])
        self.pools[pool] = out

    def _commit_read(self, pool, lo, hi, ev):
        segs = self.pools.get(pool)
        if not segs:
            self.pools[pool] = [[lo, hi, None, [ev]]]
            return
        covered = False
        for s in segs:
            if s[0] < hi and s[1] > lo:
                s[3].append(ev)
                covered = True
        if not covered:
            segs.append([lo, hi, None, [ev]])

    def op(self, eng, emit, reads=(), writes=(), dma=False, tag=""):
        assert not self.frozen

        def norm(x):
            if isinstance(x, Tile):
                return (x.pool, x.lo, x.hi)
            if isinstance(x, DramT):
                return (x.pool, 0, 1 << 60)
            return x

        reads = [norm(x) for x in reads]
        writes = [norm(x) for x in writes]
        psr = [r for r in reads if r[0] == "psum"]
        if psr:
            reads = [r for r in reads if r[0] != "psum"]
            writes = writes + psr
        waits = self._resolve(eng, reads, writes)
        seq = self._eng_seq[eng]
        self._eng_seq[eng] += 1
        dq = DMA_QUEUES[eng] if dma else None
        slot = None
        if dq is not None:
            i = self.dma_issue[dq]
            self.dma_issue[dq] = i + 1
            slot = i % DMA_RING
            use = i // DMA_RING + 1
            if use > 1:
                key = ("d", dq, slot)
                if self.wm[eng].get(key, -1) < use - 1:
                    self.wm[eng][key] = use - 1
                    waits.append((key, use - 1))
            self.dma_use[(dq, slot)] = use
        o = _Op(eng, emit, waits, dq, tag, seq)
        o.dma_slot = slot
        self.ops.append(o)
        self._op_by_ev[("e", eng, seq)] = o
        for (key, val) in waits:
            if key[0] == "e":
                self._op_by_ev[("e", key[1], val)].inc = True
        if dq is not None:
            ev = ("d", dq, slot, self.dma_use[(dq, slot)])
        else:
            ev = ("e", eng, seq)
        for (pool, lo, hi) in writes:
            self._commit_write(pool, lo, hi, ev)
        for (pool, lo, hi) in reads:
            self._commit_read(pool, lo, hi, ev)
        return ev

    def barrier_read_all_dmas(self):
        eng = "sp"
        waits = []
        for (q, slot), n in self.dma_use.items():
            key = ("d", q, slot)
            if n > 0 and self.wm[eng].get(key, -1) < n:
                waits.append((key, n))
                self.wm[eng][key] = n
        for e in ENGINES:
            if e == eng:
                continue
            seq = self._eng_seq[e] - 1
            while seq >= 0 and self._op_by_ev[("e", e, seq)].dma_q is not None:
                seq -= 1
            if seq >= 0 and self.wm[eng].get(("e", e), -1) < seq:
                o = self._op_by_ev[("e", e, seq)]
                o.inc = True
                waits.append((("e", e), seq))
                self.wm[eng][("e", e)] = seq
        o = _Op(eng, None, waits, None, "final_barrier", self._eng_seq[eng])
        self._eng_seq[eng] += 1
        self.ops.append(o)

    def finalize(self):
        self.frozen = True
        nc = self.nc
        cum = {}
        for e in ENGINES:
            cnt = 0
            arr = []
            for o in self.ops:
                if o.eng != e:
                    continue
                if o.inc:
                    cnt += 1
                arr.append(cnt)
            cum[e] = arr
        self._sems = {}
        for e in ENGINES:
            h = nc.semaphore(f"sem_{e}").__enter__()
            self._sems[("e", e)] = h
        for q in set(DMA_QUEUES.values()):
            for s in range(DMA_RING):
                if self.dma_use.get((q, s)):
                    h = nc.semaphore(f"sem_dq{q}_{s}").__enter__()
                    self._sems[("d", q, s)] = h

        eng_handles = {
            "tensor": nc.tensor, "vector": nc.vector, "scalar": nc.scalar,
            "pool": nc.gpsimd, "sp": nc.sync,
        }
        per_eng_ops = {e: [o for o in self.ops if o.eng == e] for e in ENGINES}

        with nc.Block() as block:
            for ename in ENGINES:
                ops = per_eng_ops[ename]
                if not ops:
                    continue
                eh = eng_handles[ename]

                def emit_stream(_eng, ops=ops, eh=eh, ename=ename):
                    for o in ops:
                        for (key, val) in o.waits:
                            if key[0] == "e":
                                v = cum[key[1]][val]
                                eh.wait_ge(self._sems[key], v)
                            else:
                                eh.wait_ge(self._sems[key], 16 * val)
                        if o.emit is None:
                            continue
                        inst = o.emit(eh)
                        if o.dma_q is not None:
                            inst.then_inc(self._sems[("d", o.dma_q, o.dma_slot)], 16)
                            if o.inc:
                                raise RuntimeError("dma op with engine inc")
                        elif o.inc:
                            inst.then_inc(self._sems[("e", ename)], 1)

                getattr(block, {"tensor": "tensor", "vector": "vector",
                                "scalar": "scalar", "pool": "gpsimd",
                                "sp": "sync"}[ename])(emit_stream)
        return



class Arena:
    def __init__(self, nc, name, nbytes):
        self.nbytes = nbytes
        self.name = name
        self.t8 = nc.alloc_sbuf_tensor(name, [128, nbytes], mybir.dt.uint8)
        self.pool = name
        self.free_ptr = 0
        self.stack = []

    peak = 0

    def alloc(self, cols, dtype, parts=128, align=64):
        esz = _esize(dtype)
        nb = cols * esz
        lo = (self.free_ptr + align - 1) // align * align
        hi = lo + nb
        assert hi <= self.nbytes, f"{self.name} arena overflow: {hi} > {self.nbytes}"
        self.free_ptr = hi
        Arena.peak = max(Arena.peak, hi)
        return ArenaTile(self, lo, parts, cols, dtype)

    def mark(self):
        self.stack.append(self.free_ptr)

    def release(self):
        self.free_ptr = self.stack.pop()


class ArenaTile(Tile):
    def __init__(self, arena, lo, parts, cols, dtype):
        esz = _esize(dtype)
        assert lo % esz == 0
        Tile.__init__(self, arena.t8, arena.pool, lo, lo + cols * esz,
                      0, parts, lo // esz, cols, dtype)


T = 4096
C = 1024
NT = T // 128
NC = C // 128
EPS = 1e-5


class Ctx:
    def __init__(self, nc, tk, arena):
        self.nc = nc
        self.tk = tk
        self.A = arena
        self.ps_mmA = nc.alloc_psum_tensor("ps_mmA", [128, 1024], F32)
        self.ps_mmB = nc.alloc_psum_tensor("ps_mmB", [128, 1024], F32)
        self.ps_wide = nc.alloc_psum_tensor("ps_wide", [128, 2048], F32)
        self._ps = {"A": (self.ps_mmA, 0), "B": (self.ps_mmB, 2), "W": (self.ps_wide, 4)}
        self.sq_dump = arena.alloc(1024, BF16)
        self.stat = arena.alloc(64, F32)
        self.ident = arena.alloc(128, BF16)
        self.eqm = arena.alloc(128, F32)
        self._made_ident = False
        self._consts = {}
        self._const_area = arena.alloc(32, F32)
        self._const_n = 0

    def constf(self, val):
        val = float(val)
        if val in self._consts:
            return self._consts[val]
        c = self._const_area.cview(self._const_n, 1)
        self._const_n += 1
        self.tk.op("pool", lambda e: e.memset(c.ap(), val), writes=[c], tag="const")
        self.nc.const_aps.register(F32, val, c.ap())
        self._consts[val] = c
        return c

    _PS_SPAN = {"A": (0, 4096), "B": (4096, 8192), "W": (8192, 16384)}

    def ps(self, region, col0, ncols, dtype=F32):
        t, _ = self._ps[region]
        lo, hi = Ctx._PS_SPAN[region]
        return Tile(t, "psum", lo, hi, 0, 128, col0, ncols, dtype)

    def make_ident(self):
        if self._made_ident:
            return
        self._made_ident = True
        tk, A = self.tk, self.A
        idx = A.alloc(128, F32)
        eqm = self.eqm
        tk.op("pool", lambda e: e.iota(idx.ap(), [[1, 128]], base=0, channel_multiplier=-1,
                                       allow_small_or_imprecise_dtypes=True), writes=[idx], tag="ident_iota")
        tk.op("vector", lambda e: e.tensor_single_scalar(eqm.ap(), idx.ap(), 0.0, ALU.is_equal),
              reads=[idx], writes=[eqm], tag="ident_eq")
        tk.op("vector", lambda e: e.tensor_copy(self.ident.ap(), eqm.ap()),
              reads=[eqm], writes=[self.ident], tag="ident_cp")



def bcast_row(ctx, dst_tile, dram_t, row_ap):
    ctx.tk.op("sp", lambda e: e.dma_start(dst_tile.ap(), row_ap.broadcast_to((128, dst_tile.cols))),
              reads=[dram_t], writes=[dst_tile], dma=True, tag="bcast")


def emit_ln(ctx, xt, out_tile, stat_col0, affine=None):
    tk = ctx.tk
    st = ctx.stat
    s_sum = st.cview(stat_col0 + 0, 1)
    s_ssq = st.cview(stat_col0 + 1, 1)
    s_mean = st.cview(stat_col0 + 2, 1)
    s_var = st.cview(stat_col0 + 3, 1)
    s_rstd = st.cview(stat_col0 + 4, 1)
    n = xt.cols
    tk.op("vector", lambda e: e.reduce_sum(s_sum.ap(), xt.ap(), AX.X), reads=[xt], writes=[s_sum], tag="ln_sum")
    tk.op("scalar", lambda e: e.activation(ctx.sq_dump.cview(0, n).ap(), xt.ap(), AF.Square,
                                           accum_out=s_ssq.ap()),
          reads=[xt], writes=[ctx.sq_dump, s_ssq], tag="ln_ssq")
    tk.op("vector", lambda e: e.tensor_single_scalar(s_mean.ap(), s_sum.ap(), 1.0 / n, ALU.mult),
          reads=[s_sum], writes=[s_mean], tag="ln_mean")
    tk.op("vector", lambda e: e.tensor_mul(s_var.ap(), s_mean.ap(), s_mean.ap()),
          reads=[s_mean], writes=[s_var], tag="ln_msq")
    tk.op("vector", lambda e: e.scalar_tensor_tensor(s_var.ap(), s_ssq.ap(), 1.0 / n, s_var.ap(),
                                                     ALU.mult, ALU.subtract),
          reads=[s_ssq, s_var], writes=[s_var], tag="ln_var")
    ceps = ctx.constf(EPS)
    tk.op("scalar", lambda e: e.activation(s_rstd.ap(), s_var.ap(), AF.Ln, bias=EPS),
          reads=[s_var, ceps], writes=[s_rstd], tag="ln_ln")
    tk.op("scalar", lambda e: e.activation(s_rstd.ap(), s_rstd.ap(), AF.Exp, scale=-0.5),
          reads=[s_rstd], writes=[s_rstd], tag="ln_exp")
    tk.op("vector", lambda e: e.tensor_scalar(out_tile.ap(), xt.ap(), s_mean.ap(), s_rstd.ap(),
                                              ALU.subtract, ALU.mult),
          reads=[xt, s_mean, s_rstd], writes=[out_tile], tag="ln_apply")
    if affine is not None:
        w_t, b_t = affine
        tk.op("vector", lambda e: e.tensor_mul(out_tile.ap(), out_tile.ap(), w_t.ap()),
              reads=[out_tile, w_t], writes=[out_tile], tag="ln_w")
        tk.op("vector", lambda e: e.tensor_add(out_tile.ap(), out_tile.ap(), b_t.ap()),
              reads=[out_tile, b_t], writes=[out_tile], tag="ln_b")


def emit_transpose_rt(ctx, dstT, src_tiles, gstride, psbank_region="B"):
    tk = ctx.tk
    ctx.make_ident()
    nt = len(src_tiles)
    for i in range(nt):
        pt = ctx.ps(psbank_region, (i % 2) * 1024, 1024, BF16)
        for j in range(NC):
            src = src_tiles[i].cview(j * 128, 128)
            dst = pt.cview(j * 128, 128)
            tk.op("tensor", lambda e, s=src, d=dst: e.transpose(d.ap(), s.ap(), ctx.ident.ap()),
                  reads=[src, ctx.ident], writes=[dst], tag="tr")

        def emit_evict(e, pt=pt, i=i):
            src_ap = pt.ap().rearrange("p (j c) -> p j c", j=NC)
            dst_ap = dstT.ap().rearrange("p (j g) -> p j g", j=NC)[:, :, i * 128:(i + 1) * 128]
            return e.tensor_copy(dst_ap, src_ap)

        tk.op("vector", emit_evict, reads=[pt], writes=[dstT], tag="tr_ev")


def emit_mm_acc(ctx, psum_tile, pairs, tag="mm"):
    tk = ctx.tk
    nk = len(pairs)
    for k, (lt, rt) in enumerate(pairs):
        tk.op("tensor", lambda e, lt=lt, rt=rt, k=k: e.matmul(
            psum_tile.ap(), lt.ap(), rt.ap(), start=(k == 0), stop=(k == nk - 1)),
            reads=[lt, rt], writes=[psum_tile], tag=tag)


def emit_transpose_tile(ctx, dstT, gstride, src_tile, i, psbank_region="B"):
    tk = ctx.tk
    ctx.make_ident()
    pt = ctx.ps("B" if i % 2 == 0 else "W", 0, 1024, BF16)
    for j in range(NC):
        src = src_tile.cview(j * 128, 128)
        dst = pt.cview(j * 128, 128)
        tk.op("tensor", lambda e, s=src, d=dst: e.transpose(d.ap(), s.ap(), ctx.ident.ap()),
              reads=[src, ctx.ident], writes=[dst], tag="tr")

    def emit_evict(e, pt=pt, i=i):
        src_ap = pt.ap().rearrange("p (j c) -> p j c", j=NC)
        dst_ap = dstT.ap().rearrange("p (j g) -> p j g", j=NC)[:, :, i * 128:(i + 1) * 128]
        return e.tensor_copy(dst_ap, src_ap)

    tk.op("vector", emit_evict, reads=[pt], writes=[dstT], tag="tr_ev")



DEBUG_PHASE = 99


def emit_stage_ffn0(ctx, d_xin, d_xout, W, alpha=0.5):
    tk, A = ctx.tk, ctx.A
    TH = T // 2
    NTH = TH // 128
    NF = TH // 512

    A.mark()
    wbufs = {}
    for name in ("wrT", "wkT", "wvT"):
        wt = A.alloc(NC * 1024, BF16)
        dram = W[name]
        tk.op("pool", lambda e, wt=wt, dram=dram: e.dma_start(
            wt.ap().rearrange("p (j d) -> p j d", j=NC),
            dram.t[:, :].rearrange("(j p) d -> p j d", p=128)),
            reads=[dram], writes=[wt], dma=True, tag="wload")
        wbufs[name] = wt
    mur = A.alloc(8, F32)
    muk = A.alloc(8, F32)
    tk.op("sp", lambda e: e.dma_start(mur.ap(), W["mur_pj"].t[:, :]), reads=[W["mur_pj"]], writes=[mur], dma=True)
    tk.op("sp", lambda e: e.dma_start(muk.ap(), W["muk_pj"].t[:, :]), reads=[W["muk_pj"]], writes=[muk], dma=True)

    S = [A.alloc(NC * TH, BF16) for _ in range(4)]
    pcol = A.alloc(8, BF16)
    pcol_z = A.alloc(8, BF16)
    tk.op("pool", lambda e: e.memset(pcol_z.ap(), 0.0), writes=[pcol_z], tag="pz")
    ttmp = A.alloc(2048, BF16)
    xt_db = [A.alloc(1024, F32) for _ in range(2)]
    ybf_db = [A.alloc(1024, BF16) for _ in range(2)]
    kq_db = [A.alloc(512, BF16) for _ in range(2)]
    xo_db = [A.alloc(1024, F32) for _ in range(2)]

    for half in range(2):
        t0 = half * TH
        yT, xrT, xkT, srT = S[0], S[1], S[2], S[3]
        for i in range(NTH):
            xt = xt_db[i % 2]
            ybf = ybf_db[i % 2]
            row = t0 + i * 128
            tk.op("sp", lambda e, xt=xt, row=row: e.dma_start(
                xt.ap(), d_xin.t[row:row + 128, :]), reads=[d_xin], writes=[xt], dma=True, tag="xload")
            emit_ln(ctx, xt, ybf, (i % 2) * 6)
            emit_transpose_tile(ctx, yT, TH, ybf, i)
        if DEBUG_PHASE <= 1:
            continue
        pc = pcol_z if half == 0 else pcol
        for j in range(NC):
            yj = yT.cview(j * TH, TH)
            xrj = xrT.cview(j * TH, TH)
            xkj = xkT.cview(j * TH, TH)
            murj = mur.cview(j, 1)
            mukj = muk.cview(j, 1)
            pcj = pc.cview(j, 1)
            tk.op("vector", lambda e, yj=yj: e.tensor_sub(ttmp.cview(1, TH - 1).ap(),
                                                          yj.cview(1, TH - 1).ap(), yj.cview(0, TH - 1).ap()),
                  reads=[yj], writes=[ttmp], tag="mixd")
            tk.op("vector", lambda e, yj=yj, pcj=pcj: e.tensor_sub(ttmp.cview(0, 1).ap(),
                                                                   yj.cview(0, 1).ap(), pcj.ap()),
                  reads=[yj, pc], writes=[ttmp], tag="mixd0")
            for dst, mcol in ((xrj, murj), (xkj, mukj)):
                tk.op("vector", lambda e, dst=dst, mcol=mcol, yj=yj: e.scalar_tensor_tensor(
                    dst.cview(1, TH - 1).ap(), ttmp.cview(1, TH - 1).ap(), mcol.ap(),
                    yj.cview(0, TH - 1).ap(), ALU.mult, ALU.add),
                    reads=[ttmp, mcol, yj], writes=[dst], tag="mix")
                tk.op("vector", lambda e, dst=dst, mcol=mcol, pcj=pcj: e.scalar_tensor_tensor(
                    dst.cview(0, 1).ap(), ttmp.cview(0, 1).ap(), mcol.ap(), pcj.ap(),
                    ALU.mult, ALU.add),
                    reads=[ttmp, mcol, pc], writes=[dst], tag="mix0")
        if half == 0:
            def sv(e):
                src = yT.ap().rearrange("p (j g) -> p j g", j=NC)[:, :, TH - 1:TH]
                return e.tensor_copy(pcol.ap().rearrange("p (j o) -> p j o", j=NC), src)
            tk.op("vector", sv, reads=[yT], writes=[pcol], tag="pcol")

        if DEBUG_PHASE <= 2:
            continue

        def mm_stage(wname, rhsT, evict):
            wt = wbufs[wname]
            for dp in range(NC):
                for f in range(NF):
                    par = (dp * NF + f) % 2
                    ps = ctx.ps("A" if par == 0 else "B", 0, 512)
                    pairs = [(wt.cview(jc * 1024 + dp * 128, 128), rhsT.cview(jc * TH + f * 512, 512))
                             for jc in range(NC)]
                    emit_mm_acc(ctx, ps, pairs, tag=wname)
                    evict(ps, dp, f)

        def ev_sig(ps, dp, f):
            dst = srT.cview(dp * TH + f * 512, 512)
            tk.op("scalar", lambda e, ps=ps, dst=dst: e.activation(dst.ap(), ps.ap(), AF.Sigmoid),
                  reads=[ps], writes=[dst], tag="sig")
        mm_stage("wrT", xrT, ev_sig)
        if DEBUG_PHASE <= 3:
            continue
        kact = yT
        def ev_k(ps, dp, f):
            kq = kq_db[(dp * NF + f) % 2]
            dst = kact.cview(dp * TH + f * 512, 512)
            tk.op("vector", lambda e, ps=ps, kq=kq: e.tensor_single_scalar(kq.ap(), ps.ap(), 0.0, ALU.max),
                  reads=[ps], writes=[kq], tag="relu")
            tk.op("scalar", lambda e, kq=kq, dst=dst: e.activation(dst.ap(), kq.ap(), AF.Square),
                  reads=[kq], writes=[dst], tag="sq")
        mm_stage("wkT", xkT, ev_k)
        if DEBUG_PHASE <= 4:
            continue
        outTb = xrT
        def ev_v(ps, dp, f):
            sl = dp * TH + f * 512
            tk.op("vector", lambda e, ps=ps, sl=sl: e.tensor_mul(
                outTb.cview(sl, 512).ap(), ps.ap(), srT.cview(sl, 512).ap()),
                reads=[ps, srT], writes=[outTb], tag="gate")
        mm_stage("wvT", kact, ev_v)
        if DEBUG_PHASE <= 5:
            continue

        for i in range(NTH):
            pt = ctx.ps("B" if i % 2 == 0 else "W", 0, 1024, BF16)
            for j in range(NC):
                src = outTb.cview(j * TH + i * 128, 128)
                dst = pt.cview(j * 128, 128)
                tk.op("tensor", lambda e, s=src, d=dst: e.transpose(d.ap(), s.ap(), ctx.ident.ap()),
                      reads=[src, ctx.ident], writes=[dst], tag="btr")
            xt = xt_db[i % 2]
            xo = xo_db[i % 2]
            row = t0 + i * 128
            tk.op("sp", lambda e, xt=xt, row=row: e.dma_start(
                xt.ap(), d_xin.t[row:row + 128, :]), reads=[d_xin], writes=[xt], dma=True, tag="xload2")
            tk.op("vector", lambda e, pt=pt, xt=xt, xo=xo: e.scalar_tensor_tensor(
                xo.ap(), pt.ap(), float(alpha), xt.ap(), ALU.mult, ALU.add),
                reads=[pt, xt], writes=[xo], tag="resadd")
            tk.op("sp", lambda e, xo=xo, row=row: e.dma_start(
                d_xout.t[row:row + 128, :], xo.ap()), reads=[xo], writes=[d_xout], dma=True, tag="xstore")
    A.release()



def emit_stage_moe(ctx, d_xin, d_xout, W, alpha=0.5):
    tk, A = ctx.tk, ctx.A
    NE = 8
    DFF = 4096
    NDP = DFF // 128
    TCH = 512
    NTC = T // TCH
    d_yt, d_wt, d_acc = W["d_yt"], W["d_wt"], W["d_acc"]

    A.mark()
    ctx.make_ident()
    xt_db = [A.alloc(1024, F32) for _ in range(2)]
    scol = A.alloc(32, F32)
    b2sb = A.alloc(1024, BF16, parts=8)
    wall = A.alloc(4096, BF16, parts=8)
    A.mark()
    ytmp_db = [A.alloc(1024, F32) for _ in range(2)]
    ybf_db = [A.alloc(1024, BF16) for _ in range(2)]
    ytev_db = [A.alloc(1024, BF16) for _ in range(2)]
    for i in range(NT):
        xt = xt_db[i % 2]
        yt = ytmp_db[i % 2]
        ybf = ybf_db[i % 2]
        row = i * 128
        tk.op("sp", lambda e, xt=xt, row=row: e.dma_start(
            xt.ap(), d_xin.t[row:row + 128, :]), reads=[d_xin], writes=[xt], dma=True, tag="xload")
        emit_ln(ctx, xt, yt, (i % 2) * 6)
        sc = scol.cview(i, 1)
        tk.op("vector", lambda e, yt=yt, sc=sc: e.tensor_add(sc.ap(), yt.cview(0, 1).ap(), yt.cview(1, 1).ap()),
              reads=[yt], writes=[sc], tag="scol")
        tk.op("vector", lambda e, yt=yt, ybf=ybf: e.tensor_copy(ybf.ap(), yt.ap()),
              reads=[yt], writes=[ybf], tag="ycast")
        pt = ctx.ps("B" if i % 2 == 0 else "W", 0, 1024, BF16)
        for j in range(NC):
            src = ybf.cview(j * 128, 128)
            dst = pt.cview(j * 128, 128)
            tk.op("tensor", lambda e, s=src, d=dst: e.transpose(d.ap(), s.ap(), ctx.ident.ap()),
                  reads=[src, ctx.ident], writes=[dst], tag="tr")
        ye = ytev_db[i % 2]
        tk.op("vector", lambda e, pt=pt, ye=ye: e.tensor_copy(ye.ap(), pt.ap()),
              reads=[pt], writes=[ye], tag="trev")
        tk.op("sp", lambda e, ye=ye, i=i: e.dma_start(
            d_yt.t[:, :].rearrange("p (j g) -> p j g", j=NC)[:, :, i * 128:(i + 1) * 128],
            ye.ap().rearrange("p (j c) -> p j c", j=NC)),
            reads=[ye], writes=[d_yt], dma=True, tag="ytout")

    def rint(dst, src, itmp):
        tk.op("vector", lambda e: e.tensor_copy(itmp.ap(), src.ap()), reads=[src], writes=[itmp], tag="cvt1")
        tk.op("vector", lambda e: e.tensor_copy(dst.ap(), itmp.ap()), reads=[itmp], writes=[dst], tag="cvt2")

    def floor_(dst, src, itmp, corr):
        rint(dst, src, itmp)
        tk.op("vector", lambda e: e.tensor_tensor(corr.ap(), dst.ap(), src.ap(), ALU.is_gt),
              reads=[dst, src], writes=[corr], tag="fgt")
        tk.op("vector", lambda e: e.tensor_sub(dst.ap(), dst.ap(), corr.ap()),
              reads=[dst, corr], writes=[dst], tag="fsub")

    sabs = A.alloc(32, F32)
    sgn = A.alloc(32, F32)
    fl = A.alloc(32, F32)
    itmp = A.alloc(32, I32)
    corr = A.alloc(32, F32)
    trc = A.alloc(32, F32)
    hb = A.alloc(32, F32)
    tk.op("scalar", lambda e: e.activation(sabs.ap(), scol.ap(), AF.Abs), reads=[scol], writes=[sabs], tag="habs")
    tk.op("scalar", lambda e: e.activation(sgn.ap(), scol.ap(), AF.Sign), reads=[scol], writes=[sgn], tag="hsgn")
    floor_(fl, sabs, itmp, corr)
    tk.op("vector", lambda e: e.tensor_mul(trc.ap(), fl.ap(), sgn.ap()), reads=[fl, sgn], writes=[trc], tag="htr")
    tk.op("vector", lambda e: e.tensor_single_scalar(hb.ap(), trc.ap(), 0.125, ALU.mult),
          reads=[trc], writes=[hb], tag="hdiv")
    floor_(fl, hb, itmp, corr)
    tk.op("vector", lambda e: e.scalar_tensor_tensor(hb.ap(), fl.ap(), -8.0, trc.ap(), ALU.mult, ALU.add),
          reads=[fl, trc], writes=[hb], tag="hmod")
    wmask = A.alloc(256, BF16)
    eq1 = A.alloc(32, F32)
    eq2 = A.alloc(32, F32)
    for e_ in range(NE):
        e2 = (e_ - 1) % NE
        tk.op("vector", lambda e, e_=e_: e.tensor_single_scalar(eq1.ap(), hb.ap(), float(e_), ALU.is_equal),
              reads=[hb], writes=[eq1], tag="meq1")
        tk.op("vector", lambda e, e2=e2: e.tensor_single_scalar(eq2.ap(), hb.ap(), float(e2), ALU.is_equal),
              reads=[hb], writes=[eq2], tag="meq2")
        dst = wmask.cview(e_ * 32, 32)
        tk.op("vector", lambda e, dst=dst: e.tensor_add(eq1.ap(), eq1.ap(), eq2.ap()),
              reads=[eq1, eq2], writes=[eq1], tag="madd")
        tk.op("vector", lambda e, dst=dst: e.tensor_single_scalar(dst.ap(), eq1.ap(), 0.5, ALU.mult),
              reads=[eq1], writes=[dst], tag="mhalf")
    wtev = A.alloc(128, BF16)
    for b in range(2):
        pt = ctx.ps("B", 0, 128, BF16)
        src = wmask.cview(b * 128, 128)
        tk.op("tensor", lambda e, src=src, pt=pt: e.transpose(pt.ap(), src.ap(), ctx.ident.ap()),
              reads=[src, ctx.ident], writes=[pt], tag="wtr")
        tk.op("vector", lambda e, pt=pt: e.tensor_copy(wtev.ap(), pt.ap()), reads=[pt], writes=[wtev], tag="wtev")
        tk.op("sp", lambda e, b=b: e.dma_start(d_wt.t[b * 128:(b + 1) * 128, :], wtev.ap()),
              reads=[wtev], writes=[d_wt], dma=True, tag="wtout")

    tk.op("pool", lambda e: e.dma_start(b2sb.ap(), W["moe_b2"].t[:, :]),
          reads=[W["moe_b2"]], writes=[b2sb], dma=True, tag="b2load")
    tk.op("sp", lambda e: e.dma_start(wall.ap(), d_wt.t[:, :].rearrange("(e f) q -> e (f q)", f=32)),
          reads=[d_wt], writes=[wall], dma=True, tag="wallload")

    A.release()
    A.mark()
    w1sb = A.alloc(NC * DFF, BF16)
    w2sb = A.alloc(NDP * 1024, BF16)
    b1sb = A.alloc(32, F32)
    ff1buf = A.alloc(NDP * TCH, BF16)
    ysl_db = [A.alloc(NC * TCH, BF16)]
    wrep_db = [A.alloc(TCH, BF16) for _ in range(2)]
    rtmp_db = [A.alloc(TCH, BF16) for _ in range(2)]
    obuf_db = [A.alloc(TCH, F32) for _ in range(2)]

    for e_ in range(NE):
        tk.op("pool", lambda e, e_=e_: e.dma_start(
            w1sb.ap().rearrange("p (j d) -> p j d", j=NC),
            W["moe_w1"].t[e_, :, :].rearrange("(j p) d -> p j d", p=128)),
            reads=[W["moe_w1"]], writes=[w1sb], dma=True, tag="w1load")
        tk.op("pool", lambda e, e_=e_: e.dma_start(
            w2sb.ap().rearrange("p (j m) -> p j m", j=NDP),
            W["moe_w2"].t[e_, :, :].rearrange("(j p) m -> p j m", p=128)),
            reads=[W["moe_w2"]], writes=[w2sb], dma=True, tag="w2load")
        tk.op("sp", lambda e, e_=e_: e.dma_start(b1sb.ap(), W["moe_b1pj"].t[e_, :, :]),
              reads=[W["moe_b1pj"]], writes=[b1sb], dma=True, tag="b1load")
        for tc in range(NTC):
            t0 = tc * TCH
            ysl = ysl_db[0]
            wrep = wrep_db[tc % 2]
            tk.op("sp", lambda e, ysl=ysl, t0=t0: e.dma_start(
                ysl.ap().rearrange("p (j c) -> p j c", j=NC),
                d_yt.t[:, :].rearrange("p (j g) -> p j g", j=NC)[:, :, t0:t0 + TCH]),
                reads=[d_yt], writes=[ysl], dma=True, tag="yslice")
            tk.op("sp", lambda e, wrep=wrep, e_=e_, t0=t0: e.dma_start(
                wrep.ap(),
                d_wt.t[:, :].rearrange("r q -> (r q)")[e_ * T + t0:e_ * T + t0 + TCH]
                .unsqueeze(0).broadcast_to((128, TCH))),
                reads=[d_wt], writes=[wrep], dma=True, tag="wrep")
            for dp in range(NDP):
                ps = ctx.ps("A" if dp % 2 == 0 else "B", 0, 512)
                pairs = [(w1sb.cview(jc * DFF + dp * 128, 128), ysl.cview(jc * TCH, TCH))
                         for jc in range(NC)]
                emit_mm_acc(ctx, ps, pairs, tag="ff1")
                rt = rtmp_db[dp % 2]
                b1c = b1sb.cview(dp, 1)
                tk.op("scalar", lambda e, ps=ps, rt=rt, b1c=b1c: e.activation(
                    rt.ap(), ps.ap(), AF.Relu, bias=b1c.ap()),
                    reads=[ps, b1sb], writes=[rt], tag="ff1relu")
                dst = ff1buf.cview(dp * TCH, TCH)
                tk.op("vector", lambda e, rt=rt, dst=dst, wrep=wrep: e.tensor_mul(
                    dst.ap(), rt.ap(), wrep.ap()),
                    reads=[rt, wrep], writes=[dst], tag="ff1mask")
            for mp in range(NC):
                ps = ctx.ps("A" if mp % 2 == 0 else "B", 0, 512)
                pairs = [(w2sb.cview(jc * 1024 + mp * 128, 128), ff1buf.cview(jc * TCH, TCH))
                         for jc in range(NDP)]
                if e_ == NE - 1:
                    pairs.append((b2sb.cview(mp * 128, 128), wall.cview(t0, TCH)))
                emit_mm_acc(ctx, ps, pairs, tag="ff2")
                ob = obuf_db[mp % 2]
                tk.op("vector", lambda e, ps=ps, ob=ob: e.tensor_copy(ob.ap(), ps.ap()),
                      reads=[ps], writes=[ob], tag="ff2ev")
                accsl = d_acc.t[:, mp * T + t0:mp * T + t0 + TCH]
                if e_ == 0:
                    tk.op("pool", lambda e, ob=ob, accsl=accsl: e.dma_start(accsl, ob.ap()),
                          reads=[ob], writes=[d_acc], dma=True, tag="acc0")
                else:
                    tk.op("pool", lambda e, ob=ob, accsl=accsl: e.dma_start(
                        accsl, ob.ap(), accum_op=ALU.add),
                        reads=[ob, d_acc], writes=[d_acc], dma=True, tag="accadd")

    A.release()
    accbf = A.alloc(NC * T, BF16)
    for mp in range(NC):
        tk.op("pool", lambda e, mp=mp: e.dma_start(
            accbf.cview(mp * T, T).ap(), d_acc.t[:, mp * T:(mp + 1) * T]),
            reads=[d_acc], writes=[accbf], dma=True, tag="accload")
    xo_db = [A.alloc(1024, F32) for _ in range(2)]
    for i in range(NT):
        pt = ctx.ps("B" if i % 2 == 0 else "W", 0, 1024, BF16)
        for j in range(NC):
            src = accbf.cview(j * T + i * 128, 128)
            dst = pt.cview(j * 128, 128)
            tk.op("tensor", lambda e, s=src, d=dst: e.transpose(d.ap(), s.ap(), ctx.ident.ap()),
                  reads=[src, ctx.ident], writes=[dst], tag="btr")
        xt = xt_db[i % 2]
        xo = xo_db[i % 2]
        row = i * 128
        tk.op("sp", lambda e, xt=xt, row=row: e.dma_start(
            xt.ap(), d_xin.t[row:row + 128, :]), reads=[d_xin], writes=[xt], dma=True, tag="xload2")
        tk.op("vector", lambda e, pt=pt, xt=xt, xo=xo: e.scalar_tensor_tensor(
            xo.ap(), pt.ap(), float(alpha), xt.ap(), ALU.mult, ALU.add),
            reads=[pt, xt], writes=[xo], tag="resadd")
        tk.op("sp", lambda e, xo=xo, row=row: e.dma_start(
            d_xout.t[row:row + 128, :], xo.ap()), reads=[xo], writes=[d_xout], dma=True, tag="xstore")
    A.release()



ATTN_L = 99


def emit_stage_attn(ctx, d_xin, d_xout, W):
    tk, A = ctx.tk, ctx.A
    A.mark()
    ctx.make_ident()
    d_bh = W["b_h"]
    d_bh2 = W["b_h2"]
    d_bqt = W["b_qt"]
    d_bpt = W["b_pt"]
    d_rel = W["b_relt"]

    xt_db = [A.alloc(1024, F32) for _ in range(2)]
    yt_db = [A.alloc(1024, F32) for _ in range(2)]

    for i in range(NT):
        xt = xt_db[i % 2]
        yt = yt_db[i % 2]
        row = i * 128
        tk.op("sp", lambda e, xt=xt, row=row: e.dma_start(
            xt.ap(), d_xin.t[row:row + 128, :]), reads=[d_xin], writes=[xt], dma=True, tag="xload")
        emit_ln(ctx, xt, yt, (i % 2) * 6)
        tk.op("sp", lambda e, yt=yt, row=row: e.dma_start(
            d_bh.t[row:row + 128, :], yt.ap()), reads=[yt], writes=[d_bh], dma=True, tag="h0out")

    for l in range(min(6, ATTN_L)):
        A.mark()
        ktT = A.alloc(NC * T, BF16)
        A.mark()
        hT = A.alloc(NC * T, BF16)
        hbf_db = [A.alloc(1024, BF16) for _ in range(2)]
        for i in range(NT):
            hbf = hbf_db[i % 2]
            tk.op("pool", lambda e, hbf=hbf, i=i: e.dma_start(
                hbf.ap(), d_bh.t[i * 128:(i + 1) * 128, :]),
                reads=[d_bh], writes=[hbf], dma=True, tag="hbfload")
            emit_transpose_tile(ctx, hT, T, hbf, i)
        wq = A.alloc(NC * 1024, BF16)
        wk = A.alloc(NC * 1024, BF16)
        for nm, wt in (("b_wqT", wq), ("b_wkT", wk)):
            tk.op("pool", lambda e, nm=nm, wt=wt, l=l: e.dma_start(
                wt.ap().rearrange("p (j d) -> p j d", j=NC),
                W[nm].t[l, :, :].rearrange("(j p) d -> p j d", p=128)),
                reads=[W[nm]], writes=[wt], dma=True, tag="wqk")
        bqc = A.alloc(8, F32)
        bkc = A.alloc(8, F32)
        tk.op("sp", lambda e, l=l: e.dma_start(bqc.ap(), W["b_bqpj"].t[l, :, :]),
              reads=[W["b_bqpj"]], writes=[bqc], dma=True)
        tk.op("sp", lambda e, l=l: e.dma_start(bkc.ap(), W["b_bkpj"].t[l, :, :]),
              reads=[W["b_bkpj"]], writes=[bkc], dma=True)
        qtev_db = [A.alloc(512, BF16) for _ in range(2)]
        for dp in range(NC):
            for f in range(8):
                par = (dp * 8 + f) % 2
                ps = ctx.ps("A" if par == 0 else "B", 0, 512)
                pairs = [(wq.cview(jc * 1024 + dp * 128, 128), hT.cview(jc * T + f * 512, 512))
                         for jc in range(NC)]
                emit_mm_acc(ctx, ps, pairs, tag="qproj")
                qte = qtev_db[par]
                tk.op("vector", lambda e, ps=ps, qte=qte, dp=dp: e.tensor_scalar(
                    qte.ap(), ps.ap(), bqc.cview(dp, 1).ap(), 0.125, ALU.add, ALU.mult),
                    reads=[ps, bqc], writes=[qte], tag="qev")
                tk.op("sp", lambda e, qte=qte, dp=dp, f=f: e.dma_start(
                    d_bqt.t[dp * 128:(dp + 1) * 128, f * 512:(f + 1) * 512], qte.ap()),
                    reads=[qte], writes=[d_bqt], dma=True, tag="qtout")
        for dp in range(NC):
            for f in range(8):
                par = (dp * 8 + f) % 2
                ps = ctx.ps("A" if par == 0 else "B", 0, 512)
                pairs = [(wk.cview(jc * 1024 + dp * 128, 128), hT.cview(jc * T + f * 512, 512))
                         for jc in range(NC)]
                emit_mm_acc(ctx, ps, pairs, tag="kproj")
                dst = ktT.cview(dp * T + f * 512, 512)
                tk.op("vector", lambda e, ps=ps, dst=dst, dp=dp: e.tensor_scalar(
                    dst.ap(), ps.ap(), bkc.cview(dp, 1).ap(), None, ALU.add),
                    reads=[ps, bkc], writes=[dst], tag="kev")
        A.release()

        A.mark()
        E8 = A.alloc(8 * T, BF16)
        w2sb = Tile(A.t8, A.pool, E8.lo, E8.lo + T * 4, 0, 128, E8.lo // 4, T, F32)
        wavg = A.alloc(T, F32)
        relt = A.alloc(T, F32)
        qts_db = [A.alloc(NC * 128, BF16) for _ in range(2)]
        zparts = A.alloc(64, F32)
        zsum = A.alloc(16, F32)
        dh_all = A.alloc(8 * 128, BF16)
        pb = A.alloc(T, BF16)
        pb2 = pb
        ptev = A.alloc(T, BF16)
        z2c = A.alloc(4, F32)
        for qb in range(NT):
            qts = qts_db[qb % 2]
            tk.op("pool", lambda e, qts=qts, qb=qb: e.dma_start(
                qts.ap().rearrange("p (j c) -> p j c", j=NC),
                d_bqt.t[:, :].rearrange("(j p) t -> p j t", p=128)[:, :, qb * 128:(qb + 1) * 128]),
                reads=[d_bqt], writes=[qts], dma=True, tag="qtsload")
            tk.op("sp", lambda e, qb=qb: e.dma_start(
                relt.ap(), d_rel.t[:, 3968 - qb * 128:3968 - qb * 128 + T]),
                reads=[d_rel], writes=[relt], dma=True, tag="relload")
            for grp in range(2):
                for h8 in range(8):
                    h = grp * 8 + h8
                    lhs = qts.cview((h // 2) * 128, 128).pview((h % 2) * 64, 64)
                    for c4 in range(4):
                        ps = ctx.ps("A" if c4 % 2 == 0 else "B", 0, 1024)
                        for ff in range(2):
                            rhs = ktT.cview((h // 2) * T + c4 * 1024 + ff * 512, 512).pview((h % 2) * 64, 64)
                            pss = ps.cview(ff * 512, 512)
                            tk.op("tensor", lambda e, pss=pss, lhs=lhs, rhs=rhs: e.matmul(
                                pss.ap(), lhs.ap(), rhs.ap(), start=True, stop=True),
                                reads=[qts, ktT], writes=[pss], tag="score")
                        dst = E8.cview(h8 * T + c4 * 1024, 1024)
                        zp = zparts.cview(h8 * 4 + c4, 1)
                        tk.op("scalar", lambda e, ps=ps, dst=dst, zp=zp: e.activation(
                            dst.ap(), ps.ap(), AF.Exp, accum_out=zp.ap()),
                            reads=[ps], writes=[dst, zp], tag="exp1")
                tk.op("vector", lambda e: e.reduce_sum(
                    zsum.cview(0, 8).ap().rearrange("p (a o) -> p a o", o=1),
                    zparts.cview(0, 32).ap().rearrange("p (a b) -> p a b", a=8), AX.X),
                    reads=[zparts], writes=[zsum], tag="zred")
                tk.op("vector", lambda e: e.tensor_single_scalar(
                    zsum.cview(0, 8).ap(), zsum.cview(0, 8).ap(), 1e-37, ALU.add),
                    reads=[zsum], writes=[zsum], tag="zeps")
                tk.op("vector", lambda e: e.reciprocal(zsum.cview(8, 8).ap(), zsum.cview(0, 8).ap()),
                      reads=[zsum], writes=[zsum], tag="zrec")
                for h8 in range(8):
                    dh = dh_all.cview(h8 * 128, 128)
                    tk.op("vector", lambda e, dh=dh, h8=h8: e.tensor_scalar(
                        dh.ap(), ctx.eqm.ap(), zsum.cview(8 + h8, 1).ap(), 0.0625,
                        ALU.mult, ALU.mult),
                        reads=[ctx.eqm, zsum], writes=[dh], tag="dh")
                for half in range(2):
                    for h8 in range(8):
                        for fq in range(4):
                            o = Tile(ctx.ps_wide, "psum", 8192, 16384, 0, 128, fq * 512, 512, F32)
                            tk.op("tensor", lambda e, o=o, h8=h8, half=half, fq=fq: e.matmul(
                                o.ap(), dh_all.cview(h8 * 128, 128).ap(),
                                E8.cview(h8 * T + half * 2048 + fq * 512, 512).ap(),
                                start=(h8 == 0), stop=(h8 == 7)),
                                reads=[dh_all, E8], writes=[o], tag="diag")
                    wv = Tile(ctx.ps_wide, "psum", 8192, 16384, 0, 128, 0, 2048, F32)
                    wdst = wavg.cview(half * 2048, 2048)
                    if grp == 0:
                        tk.op("vector", lambda e, wv=wv, wdst=wdst: e.tensor_copy(wdst.ap(), wv.ap()),
                              reads=[wv], writes=[wdst], tag="wavg0")
                    else:
                        tk.op("vector", lambda e, wv=wv, wdst=wdst: e.tensor_add(
                            wdst.ap(), wdst.ap(), wv.ap()),
                            reads=[wv, wdst], writes=[wdst], tag="wavg1")
            tk.op("vector", lambda e: e.tensor_add(w2sb.ap(), wavg.ap(), relt.ap()),
                  reads=[wavg, relt], writes=[w2sb], tag="rel_add")
            tk.op("scalar", lambda e: e.activation(pb.ap(), w2sb.ap(), AF.Exp,
                                                   accum_out=z2c.cview(0, 1).ap()),
                  reads=[w2sb], writes=[pb, z2c], tag="exp2")
            tk.op("vector", lambda e: e.tensor_single_scalar(
                z2c.cview(0, 1).ap(), z2c.cview(0, 1).ap(), 1e-37, ALU.add),
                reads=[z2c], writes=[z2c], tag="z2eps")
            tk.op("vector", lambda e: e.reciprocal(z2c.cview(1, 1).ap(), z2c.cview(0, 1).ap()),
                  reads=[z2c], writes=[z2c], tag="z2rec")
            tk.op("vector", lambda e: e.tensor_scalar(pb2.ap(), pb.ap(), z2c.cview(1, 1).ap(), None, ALU.mult),
                  reads=[pb, z2c], writes=[pb2], tag="pnorm")
            for mc in range(32):
                pt = ctx.ps("B" if mc % 2 == 0 else "W", 0, 1024, BF16)
                ptc = pt.cview(0, 128)
                src = pb2.cview(mc * 128, 128)
                tk.op("tensor", lambda e, src=src, ptc=ptc: e.transpose(ptc.ap(), src.ap(), ctx.ident.ap()),
                      reads=[src, ctx.ident], writes=[ptc], tag="ptr")
                dst = ptev.cview(mc * 128, 128)
                tk.op("vector", lambda e, ptc=ptc, dst=dst: e.tensor_copy(dst.ap(), ptc.ap()),
                      reads=[ptc], writes=[dst], tag="ptev")
            tk.op("sp", lambda e, qb=qb: e.dma_start(d_bpt.t[qb, :, :], ptev.ap()),
                  reads=[ptev], writes=[d_bpt], dma=True, tag="ptout")
        A.release()
        A.release()

        A.mark()
        hrb = A.alloc(32 * 1024, BF16)
        for mc in range(NT):
            tk.op("pool", lambda e, mc=mc: e.dma_start(
                hrb.cview(mc * 1024, 1024).ap(), d_bh.t[mc * 128:(mc + 1) * 128, :]),
                reads=[d_bh], writes=[hrb], dma=True, tag="hrbload")
        lw1 = A.alloc(1024, F32)
        lb1 = A.alloc(1024, F32)
        lw2 = A.alloc(1024, F32)
        lb2 = A.alloc(1024, F32)
        for nm, tile in (("b_ln1w", lw1), ("b_ln1b", lb1), ("b_ln2w", lw2), ("b_ln2b", lb2)):
            tk.op("sp", lambda e, nm=nm, tile=tile, l=l: e.dma_start(
                tile.ap(), W[nm].t[l:l + 1, :].broadcast_to((128, 1024))),
                reads=[W[nm]], writes=[tile], dma=True, tag="lnw")
        pts_db = [A.alloc(32 * 128, BF16) for _ in range(2)]
        z1_db = [A.alloc(1024, F32) for _ in range(2)]
        h2_db = [A.alloc(1024, F32) for _ in range(2)]
        for qb in range(NT):
            pts = pts_db[qb % 2]
            tk.op("pool", lambda e, pts=pts, qb=qb: e.dma_start(pts.ap(), d_bpt.t[qb, :, :]),
                  reads=[d_bpt], writes=[pts], dma=True, tag="ptsload")
            xt = xt_db[qb % 2]
            tk.op("sp", lambda e, xt=xt, qb=qb: e.dma_start(
                xt.ap(), d_bh.t[qb * 128:(qb + 1) * 128, :]),
                reads=[d_bh], writes=[xt], dma=True, tag="hload3")
            z1 = z1_db[qb % 2]
            for eh in range(2):
                ps = ctx.ps("A" if eh == 0 else "B", 0, 512)
                pairs = [(pts.cview(mc * 128, 128), hrb.cview(mc * 1024 + eh * 512, 512))
                         for mc in range(NT)]
                emit_mm_acc(ctx, ps, pairs, tag="sa")
                tk.op("vector", lambda e, ps=ps, z1=z1, xt=xt, eh=eh: e.tensor_add(
                    z1.cview(eh * 512, 512).ap(), ps.ap(), xt.cview(eh * 512, 512).ap()),
                    reads=[ps, xt], writes=[z1], tag="saadd")
            h2t = h2_db[qb % 2]
            emit_ln(ctx, z1, z1, (qb % 2) * 6, affine=(lw1, lb1))
            emit_ln(ctx, z1, h2t, (qb % 2) * 6 + 12 - 12, affine=(lw2, lb2))
            tk.op("sp", lambda e, h2t=h2t, qb=qb: e.dma_start(
                d_bh2.t[qb * 128:(qb + 1) * 128, :], h2t.ap()),
                reads=[h2t], writes=[d_bh2], dma=True, tag="h2out")
        A.release()

        A.mark()
        w1T = A.alloc(NC * 4096, BF16)
        w2sb4 = A.alloc(32 * 1024, BF16)
        tk.op("pool", lambda e, l=l: e.dma_start(
            w1T.ap().rearrange("p (j d) -> p j d", j=NC),
            W["b_w1T"].t[l, :, :].rearrange("(j p) d -> p j d", p=128)),
            reads=[W["b_w1T"]], writes=[w1T], dma=True, tag="w1load")
        tk.op("pool", lambda e, l=l: e.dma_start(
            w2sb4.ap().rearrange("p (j m) -> p j m", j=32),
            W["b_w2T"].t[l, :, :].rearrange("(j p) m -> p j m", p=128)),
            reads=[W["b_w2T"]], writes=[w2sb4], dma=True, tag="w2load")
        b1c = A.alloc(32, F32)
        tk.op("sp", lambda e, l=l: e.dma_start(b1c.ap(), W["b_b1pj"].t[l, :, :]),
              reads=[W["b_b1pj"]], writes=[b1c], dma=True)
        b2bc = A.alloc(1024, BF16)
        tk.op("pool", lambda e, l=l: e.dma_start(
            b2bc.ap(), W["b_b2"].t[l:l + 1, :].broadcast_to((128, 1024))),
            reads=[W["b_b2"]], writes=[b2bc], dma=True, tag="b2bc")
        lw3 = A.alloc(1024, BF16)
        lb3 = A.alloc(1024, BF16)
        for nm, tile in (("b_ln3w", lw3), ("b_ln3b", lb3)):
            tk.op("pool", lambda e, nm=nm, tile=tile, l=l: e.dma_start(
                tile.ap(), W[nm].t[l:l + 1, :].broadcast_to((128, 1024))),
                reads=[W[nm]], writes=[tile], dma=True, tag="lnw3")
        h2ts = A.alloc(NC * 512, BF16)
        h2bf_db = [A.alloc(1024, BF16)]
        ff1buf = A.alloc(32 * 512, BF16)
        fft_db = [A.alloc(1024, F32)]
        h24_db = [A.alloc(1024, F32)]
        for f in range(8):
            for i4 in range(4):
                h2bf = h2bf_db[0]
                row = f * 512 + i4 * 128
                tk.op("pool", lambda e, h2bf=h2bf, row=row: e.dma_start(
                    h2bf.ap(), d_bh2.t[row:row + 128, :]),
                    reads=[d_bh2], writes=[h2bf], dma=True, tag="h2bfload")
                emit_transpose_tile(ctx, h2ts, 512, h2bf, i4)
            for dp in range(32):
                ps = ctx.ps("A" if dp % 2 == 0 else "B", 0, 512)
                pairs = [(w1T.cview(jc * 4096 + dp * 128, 128), h2ts.cview(jc * 512, 512))
                         for jc in range(NC)]
                emit_mm_acc(ctx, ps, pairs, tag="ff1")
                dst = ff1buf.cview(dp * 512, 512)
                tk.op("scalar", lambda e, ps=ps, dst=dst, dp=dp: e.activation(
                    dst.ap(), ps.ap(), AF.Relu, bias=b1c.cview(dp, 1).ap()),
                    reads=[ps, b1c], writes=[dst], tag="ff1relu")
            for tc4 in range(4):
                fft = fft_db[0]
                h24 = h24_db[0]
                hn = fft
                row = f * 512 + tc4 * 128
                tk.op("sp", lambda e, h24=h24, row=row: e.dma_start(
                    h24.ap(), d_bh2.t[row:row + 128, :]),
                    reads=[d_bh2], writes=[h24], dma=True, tag="h24load")
                for eh in range(2):
                    ps = ctx.ps("A" if eh == 0 else "B", 0, 512)
                    pairs = [(ff1buf.cview(dp * 512 + tc4 * 128, 128),
                              w2sb4.cview(dp * 1024 + eh * 512, 512)) for dp in range(32)]
                    emit_mm_acc(ctx, ps, pairs, tag="ff2")
                    tk.op("vector", lambda e, ps=ps, fft=fft, eh=eh: e.scalar_tensor_tensor(
                        fft.cview(eh * 512, 512).ap(), ps.ap(), 1.0,
                        b2bc.cview(eh * 512, 512).ap(), ALU.mult, ALU.add),
                        reads=[ps, b2bc], writes=[fft], tag="ff2ev")
                tk.op("vector", lambda e, fft=fft, h24=h24: e.tensor_add(
                    fft.ap(), fft.ap(), h24.ap()),
                    reads=[fft, h24], writes=[fft], tag="ffres")
                emit_ln(ctx, fft, hn, (tc4 % 2) * 6, affine=(lw3, lb3))
                tk.op("sp", lambda e, hn=hn, row=row: e.dma_start(
                    d_bh.t[row:row + 128, :], hn.ap()),
                    reads=[hn], writes=[d_bh], dma=True, tag="hnout")
        A.release()

    ho_db = [A.alloc(1024, F32) for _ in range(2)]
    xo_db = [A.alloc(1024, F32) for _ in range(2)]
    for i in range(NT):
        xt = xt_db[i % 2]
        ho = ho_db[i % 2]
        xo = xo_db[i % 2]
        row = i * 128
        tk.op("sp", lambda e, xt=xt, row=row: e.dma_start(
            xt.ap(), d_xin.t[row:row + 128, :]), reads=[d_xin], writes=[xt], dma=True, tag="xload9")
        tk.op("sp", lambda e, ho=ho, row=row: e.dma_start(
            ho.ap(), d_bh.t[row:row + 128, :]), reads=[d_bh], writes=[ho], dma=True, tag="hload9")
        tk.op("vector", lambda e, xt=xt, ho=ho, xo=xo: e.tensor_add(
            xo.ap(), xt.ap(), ho.ap()), reads=[xt, ho], writes=[xo], tag="fres")
        tk.op("sp", lambda e, xo=xo, row=row: e.dma_start(
            d_xout.t[row:row + 128, :], xo.ap()), reads=[xo], writes=[d_xout], dma=True, tag="xstore9")
    A.release()



DI = 2048
NJI = DI // 128


MAMBA_PH = 99
MAMBA_V2 = 0


def emit_stage_mamba(ctx, d_xin, d_xout, W):
    def mp(blk, p):
        return blk * 7 + p < MAMBA_PH
    tk, A = ctx.tk, ctx.A
    A.mark()
    ctx.make_ident()

    d_yt = W["mam_yt"]
    d_u = W["mam_u"]
    d_z = W["mam_z"]
    d_dl = W["mam_dl"]
    d_ys = W["mam_ys"]
    d_yo = W["mam_yo"]

    rrep = A.alloc(16 * 128, BF16)
    s8 = A.alloc(8 * 64, BF16)
    r16b = A.alloc(128, BF16, parts=32)
    r16c = A.alloc(128, BF16, parts=32)
    tk.op("pool", lambda e: e.dma_start(rrep.ap().rearrange("p (g c) -> p g c", g=16),
                                      W["mam_rrep"].t[:, :, :].rearrange("g c p -> c g p")),
          reads=[W["mam_rrep"]], writes=[rrep], dma=True, tag="rrep")
    tk.op("pool", lambda e: e.dma_start(s8.ap().rearrange("p (q c) -> p q c", q=8),
                                      W["mam_s8"].t[:, :, :].rearrange("q c p -> c q p")),
          reads=[W["mam_s8"]], writes=[s8], dma=True, tag="s8")
    tk.op("pool", lambda e: e.dma_start(r16b.ap(), W["mam_r16b"].t[:, :]),
          reads=[W["mam_r16b"]], writes=[r16b], dma=True)
    tk.op("pool", lambda e: e.dma_start(r16c.ap(), W["mam_r16c"].t[:, :]),
          reads=[W["mam_r16c"]], writes=[r16c], dma=True)

    xt_db = [A.alloc(1024, F32) for _ in range(2)]
    ybf_db = [A.alloc(1024, BF16) for _ in range(2)]
    ytev_db = [A.alloc(1024, BF16) for _ in range(2)]
    ytmp_db = [A.alloc(1024, F32) for _ in range(2)]

    for i in range(NT):
        xt = xt_db[i % 2]
        yt = ytmp_db[i % 2]
        ybf = ybf_db[i % 2]
        row = i * 128
        tk.op("sp", lambda e, xt=xt, row=row: e.dma_start(
            xt.ap(), d_xin.t[row:row + 128, :]), reads=[d_xin], writes=[xt], dma=True, tag="xload")
        emit_ln(ctx, xt, yt, (i % 2) * 6)
        tk.op("vector", lambda e, yt=yt, ybf=ybf: e.tensor_copy(ybf.ap(), yt.ap()),
              reads=[yt], writes=[ybf], tag="ycast")
        pt = ctx.ps("B" if i % 2 == 0 else "W", 0, 1024, BF16)
        for j in range(NC):
            src = ybf.cview(j * 128, 128)
            dst = pt.cview(j * 128, 128)
            tk.op("tensor", lambda e, s=src, d=dst: e.transpose(d.ap(), s.ap(), ctx.ident.ap()),
                  reads=[src, ctx.ident], writes=[dst], tag="tr")
        ye = ytev_db[i % 2]
        tk.op("vector", lambda e, pt=pt, ye=ye: e.tensor_copy(ye.ap(), pt.ap()),
              reads=[pt], writes=[ye], tag="trev")
        tk.op("sp", lambda e, ye=ye, i=i: e.dma_start(
            d_yt.t[:, :].rearrange("(j p) t -> p j t", p=128)[:, :, i * 128:(i + 1) * 128],
            ye.ap().rearrange("p (j c) -> p j c", j=NC)),
            reads=[ye], writes=[d_yt], dma=True, tag="ytout")

    cur = A.alloc(NC * T, BF16)
    tk.op("sp", lambda e: e.dma_start(
        cur.ap().rearrange("p (j t) -> p j t", j=NC),
        d_yt.t[:, :].rearrange("(j p) t -> p j t", p=128)),
        reads=[d_yt], writes=[cur], dma=True, tag="cur0")

    for blk in range(2):
        A.mark()
        apj = A.alloc(256, F32)
        dpj = A.alloc(16, F32)
        cwpj = A.alloc(64, F32)
        cbpj = A.alloc(16, F32)
        dtbpj = A.alloc(16, F32)
        for name, tile in (("mam_apj", apj), ("mam_dpj", dpj), ("mam_cwpj", cwpj),
                           ("mam_cbpj", cbpj), ("mam_dtbpj", dtbpj)):
            tk.op("sp", lambda e, name=name, tile=tile, blk=blk: e.dma_start(
                tile.ap(), W[name].t[blk, :, :]), reads=[W[name]], writes=[tile], dma=True, tag=name)
        acol = A.alloc(256, F32)
        tk.op("scalar", lambda e: e.activation(acol.ap(), apj.ap(), AF.Exp),
              reads=[apj], writes=[acol], tag="aexp")
        tk.op("vector", lambda e: e.tensor_single_scalar(acol.ap(), acol.ap(), -1.0, ALU.mult),
              reads=[acol], writes=[acol], tag="aneg")

        dmask = A.alloc(128, BF16)
        idx = A.alloc(128, F32)
        eqf = A.alloc(128, F32)
        tk.op("pool", lambda e: e.iota(idx.ap(), [[1, 128]], base=0, channel_multiplier=-1,
                                       allow_small_or_imprecise_dtypes=True), writes=[idx], tag="didx")
        tk.op("vector", lambda e: e.tensor_single_scalar(eqf.ap(), idx.ap(), 0.0, ALU.is_equal),
              reads=[idx], writes=[eqf], tag="deq")
        tk.op("vector", lambda e: e.tensor_copy(dmask.ap(), eqf.ap()), reads=[eqf], writes=[dmask], tag="deqc")

        if not mp(blk, 0):
            A.release()
            break
        A.mark()
        winT = A.alloc(NC * 4096, BF16)
        tk.op("pool", lambda e, blk=blk: e.dma_start(
            winT.ap().rearrange("p (j d) -> p j d", j=NC),
            W["mam_inT"].t[blk, :, :].rearrange("(j p) d -> p j d", p=128)),
            reads=[W["mam_inT"]], writes=[winT], dma=True, tag="winT")
        xcs = A.alloc(16 + T, BF16)
        usb_db = [A.alloc(T, BF16) for _ in range(2)]
        dk = [A.alloc(128, BF16) for _ in range(4)]
        tk.op("pool", lambda e: e.memset(xcs.cview(0, 16).ap(), 0.0), writes=[xcs.cview(0, 16)], tag="xcpad")
        for jo in range(NJI * 2):
            is_xc = jo < NJI
            dstb = usb_db[jo % 2] if not is_xc else None
            for f in range(8):
                ps = ctx.ps("A" if f % 2 == 0 else "B", 0, 512)
                pairs = [(winT.cview(jc * 4096 + jo * 128, 128), cur.cview(jc * T + f * 512, 512))
                         for jc in range(NC)]
                emit_mm_acc(ctx, ps, pairs, tag="inproj")
                if is_xc:
                    dst = xcs.cview(16 + f * 512, 512)
                else:
                    dst = dstb.cview(f * 512, 512)
                tk.op("vector", lambda e, ps=ps, dst=dst: e.tensor_copy(dst.ap(), ps.ap()),
                      reads=[ps], writes=[dst], tag="xzev")
            if is_xc:
                j = jo
                for k in range(4):
                    wcol = cwpj.cview(j * 4 + k, 1)
                    tk.op("vector", lambda e, k=k, wcol=wcol: e.tensor_scalar(
                        dk[k].ap(), dmask.ap(), wcol.ap(), None, ALU.mult),
                        reads=[dmask, cwpj], writes=[dk[k]], tag="dkbuild")
                usb = usb_db[j % 2]
                for f in range(8):
                    ps = ctx.ps("A" if f % 2 == 0 else "B", 0, 512)
                    pairs = [(dk[k], xcs.cview(16 + f * 512 - 3 + k, 512)) for k in range(4)]
                    emit_mm_acc(ctx, ps, pairs, tag="conv")
                    bcol = cbpj.cview(j, 1)
                    tk.op("scalar", lambda e, ps=ps, usb=usb, f=f, bcol=bcol: e.activation(
                        usb.cview(f * 512, 512).ap(), ps.ap(), AF.Silu, bias=bcol.ap()),
                        reads=[ps, cbpj], writes=[usb], tag="usilu")
                tk.op("sp", lambda e, usb=usb, j=j: e.dma_start(
                    d_u.t[j * 128:(j + 1) * 128, :], usb.ap()),
                    reads=[usb], writes=[d_u], dma=True, tag="uout")
            else:
                j = jo - NJI
                tk.op("sp", lambda e, dstb=dstb, j=j: e.dma_start(
                    d_z.t[j * 128:(j + 1) * 128, :], dstb.ap()),
                    reads=[dstb], writes=[d_z], dma=True, tag="zout")
        A.release()

        if not mp(blk, 1):
            A.release()
            break
        xd1 = A.alloc(T, BF16, parts=64)
        xd2 = A.alloc(T, BF16, parts=32)
        A.mark()
        wx1 = A.alloc(16 * 64, BF16)
        wx2 = A.alloc(16 * 32, BF16)
        tk.op("pool", lambda e, blk=blk: e.dma_start(wx1.ap(), W["mam_x1T"].t[blk, :, :]),
              reads=[W["mam_x1T"]], writes=[wx1], dma=True, tag="wx1")
        tk.op("pool", lambda e, blk=blk: e.dma_start(wx2.ap(), W["mam_x2T"].t[blk, :, :]),
              reads=[W["mam_x2T"]], writes=[wx2], dma=True, tag="wx2")
        uin_db = [A.alloc(T, BF16) for _ in range(2)]
        for f in range(8 if MAMBA_V2 != 3 else 0):
            for which in range(2):
                if MAMBA_V2 == 1 and which == 1:
                    continue
                ps = ctx.ps("A" if which == 0 else "B", 0, 512)
                for jc in range(NJI):
                    ut = uin_db[jc % 2]
                    tk.op("sp", lambda e, ut=ut, jc=jc, f=f: e.dma_start(
                        ut.cview(f * 512, 512).ap(),
                        d_u.t[jc * 128:(jc + 1) * 128, f * 512:(f + 1) * 512]),
                        reads=[d_u], writes=[ut], dma=True, tag="uload")
                    if MAMBA_V2 == 1:
                        continue
                    if MAMBA_V2 == 2:
                        ut = uin_db[0]
                    if which == 0:
                        tk.op("tensor", lambda e, ps=ps, jc=jc, ut=ut, f=f: e.matmul(
                            ps.pview(0, 64).ap(), wx1.cview(jc * 64, 64).ap(),
                            ut.cview(f * 512, 512).ap(),
                            start=(jc == 0), stop=(jc == NJI - 1)),
                            reads=[wx1, ut], writes=[ps], tag="xproj1")
                    else:
                        tk.op("tensor", lambda e, ps=ps, jc=jc, ut=ut, f=f: e.matmul(
                            ps.pview(0, 32).ap(), wx2.cview(jc * 32, 32).ap(),
                            ut.cview(f * 512, 512).ap(),
                            start=(jc == 0), stop=(jc == NJI - 1)),
                            reads=[wx2, ut], writes=[ps], tag="xproj2")
                if MAMBA_V2 == 1:
                    continue
                if which == 0:
                    tk.op("vector", lambda e, ps=ps, f=f: e.tensor_copy(
                        xd1.cview(f * 512, 512).ap(), ps.pview(0, 64).ap()),
                        reads=[ps], writes=[xd1], tag="xd1ev")
                else:
                    tk.op("vector", lambda e, ps=ps, f=f: e.tensor_copy(
                        xd2.cview(f * 512, 512).ap(), ps.pview(0, 32).ap()),
                        reads=[ps], writes=[xd2], tag="xd2ev")
        A.release()

        if not mp(blk, 2):
            A.release()
            break
        A.mark()
        wdt = A.alloc(16 * 128, BF16)
        tk.op("pool", lambda e, blk=blk: e.dma_start(
            wdt.pview(0, 64).ap(), W["mam_dtT"].t[blk, :, :]),
            reads=[W["mam_dtT"]], writes=[wdt], dma=True, tag="wdt")
        t_abs = A.alloc(512, F32)
        t_exp = A.alloc(512, F32)
        t_ln = A.alloc(512, F32)
        t_mx = A.alloc(512, F32)
        dlout_db = [A.alloc(T, BF16) for _ in range(2)]
        for j in range(NJI):
            dlo = dlout_db[j % 2]
            for f in range(8):
                ps = ctx.ps("A" if f % 2 == 0 else "B", 0, 512)
                tk.op("tensor", lambda e, ps=ps, j=j, f=f: e.matmul(
                    ps.ap(), wdt.pview(0, 64).cview(j * 128, 128).ap(), xd1.cview(f * 512, 512).ap(),
                    start=True, stop=True),
                    reads=[wdt, xd1], writes=[ps], tag="dtproj")
                bcol = dtbpj.cview(j, 1)
                if MAMBA_V2 == 4:
                    tk.op("vector", lambda e, ps=ps, dlo=dlo, f=f: e.tensor_copy(
                        dlo.cview(f * 512, 512).ap(), ps.ap()),
                        reads=[ps], writes=[dlo], tag="dtev")
                    continue
                tk.op("scalar", lambda e, ps=ps, bcol=bcol: e.activation(
                    t_abs.ap(), ps.ap(), AF.Abs, bias=bcol.ap()),
                    reads=[ps, dtbpj], writes=[t_abs], tag="sp_abs")
                tk.op("scalar", lambda e: e.activation(t_exp.ap(), t_abs.ap(), AF.Exp, scale=-1.0),
                      reads=[t_abs], writes=[t_exp], tag="sp_exp")
                cone = ctx.constf(1.0)
                tk.op("scalar", lambda e: e.activation(t_ln.ap(), t_exp.ap(), AF.Ln, bias=1.0),
                      reads=[t_exp, cone], writes=[t_ln], tag="sp_ln")
                if MAMBA_V2 == 5:
                    tk.op("vector", lambda e, dlo=dlo, f=f: e.tensor_copy(
                        dlo.cview(f * 512, 512).ap(), t_ln.ap()),
                        reads=[t_ln], writes=[dlo], tag="dtev5")
                    continue
                tk.op("vector", lambda e, ps=ps, bcol=bcol: e.tensor_scalar(
                    t_mx.ap(), ps.ap(), bcol.ap(), 0.0, ALU.add, ALU.max),
                    reads=[ps, dtbpj], writes=[t_mx], tag="sp_max")
                tk.op("vector", lambda e, dlo=dlo, f=f: e.tensor_add(
                    dlo.cview(f * 512, 512).ap(), t_mx.ap(), t_ln.ap()),
                    reads=[t_mx, t_ln], writes=[dlo], tag="sp_add")
            tk.op("sp", lambda e, dlo=dlo, j=j: e.dma_start(
                d_dl.t[j * 128:(j + 1) * 128, :], dlo.ap()),
                reads=[dlo], writes=[d_dl], dma=True, tag="dlout")
        A.release()

        if not mp(blk, 3):
            A.release()
            break
        A.mark()
        brep = A.alloc(T, BF16)
        crep = A.alloc(T, BF16)
        for f in range(8):
            psb = ctx.ps("A", 0, 512)
            tk.op("tensor", lambda e, psb=psb, f=f: e.matmul(
                psb.ap(), r16b.ap(), xd2.cview(f * 512, 512).ap(), start=True, stop=True),
                reads=[r16b, xd2], writes=[psb], tag="brep")
            tk.op("scalar", lambda e, psb=psb, f=f: e.activation(
                brep.cview(f * 512, 512).ap(), psb.ap(), AF.Copy),
                reads=[psb], writes=[brep], tag="brepev")
            psc = ctx.ps("B", 0, 512)
            tk.op("tensor", lambda e, psc=psc, f=f: e.matmul(
                psc.ap(), r16c.ap(), xd2.cview(f * 512, 512).ap(), start=True, stop=True),
                reads=[r16c, xd2], writes=[psc], tag="crep")
            tk.op("scalar", lambda e, psc=psc, f=f: e.activation(
                crep.cview(f * 512, 512).ap(), psc.ap(), AF.Copy),
                reads=[psc], writes=[crep], tag="crepev")

        SEC = 2048
        if not mp(blk, 4):
            A.release()
            A.release()
            break
        dl_j = A.alloc(SEC, BF16)
        u_j = A.alloc(SEC, BF16)
        e_j = A.alloc(SEC, BF16)
        da_g = A.alloc(SEC, BF16)
        dbu_g = A.alloc(SEC, BF16)
        h_g = A.alloc(SEC, BF16)
        ch_g = A.alloc(SEC, BF16)
        carry = A.alloc(256, F32)
        ysev = A.alloc(2048, BF16, parts=64)
        NSC = SEC // 512
        for sec in range(2):
            tsl0 = sec * SEC
            for j in range(NJI):
                tk.op("sp", lambda e, j=j, tsl0=tsl0: e.dma_start(
                    dl_j.ap(), d_dl.t[j * 128:(j + 1) * 128, tsl0:tsl0 + SEC]),
                    reads=[d_dl], writes=[dl_j], dma=True, tag="dlin")
                tk.op("sp", lambda e, j=j, tsl0=tsl0: e.dma_start(
                    u_j.ap(), d_u.t[j * 128:(j + 1) * 128, tsl0:tsl0 + SEC]),
                    reads=[d_u], writes=[u_j], dma=True, tag="uin")
                tk.op("vector", lambda e: e.tensor_mul(e_j.ap(), dl_j.ap(), u_j.ap()),
                      reads=[dl_j, u_j], writes=[e_j], tag="ej")
                for gl in range(16):
                    g = j * 16 + gl
                    for f in range(NSC):
                        psA = ctx.ps("A", 0, 512)
                        tk.op("tensor", lambda e, psA=psA, gl=gl, f=f: e.matmul(
                            psA.ap(), rrep.cview(gl * 128, 128).ap(), dl_j.cview(f * 512, 512).ap(),
                            start=True, stop=True),
                            reads=[rrep, dl_j], writes=[psA], tag="drep")
                        ac = acol.cview(g, 1)
                        tk.op("scalar", lambda e, psA=psA, f=f, ac=ac: e.activation(
                            da_g.cview(f * 512, 512).ap(), psA.ap(), AF.Exp, scale=ac.ap()),
                            reads=[psA, acol], writes=[da_g], tag="daexp")
                        psB = ctx.ps("B", 0, 512)
                        tk.op("tensor", lambda e, psB=psB, gl=gl, f=f: e.matmul(
                            psB.ap(), rrep.cview(gl * 128, 128).ap(), e_j.cview(f * 512, 512).ap(),
                            start=True, stop=True),
                            reads=[rrep, e_j], writes=[psB], tag="durep")
                        tk.op("vector", lambda e, psB=psB, f=f, tsl0=tsl0: e.tensor_mul(
                            dbu_g.cview(f * 512, 512).ap(), psB.ap(),
                            brep.cview(tsl0 + f * 512, 512).ap()),
                            reads=[psB, brep], writes=[dbu_g], tag="dbu")
                    cg = carry.cview(g, 1)
                    pass
                    if sec == 0:
                        tk.op("vector", lambda e: e.tensor_tensor_scan(
                            h_g.ap(), da_g.ap(), dbu_g.ap(), 0.0, ALU.mult, ALU.add),
                            reads=[da_g, dbu_g], writes=[h_g], tag="scan")
                    else:
                        tk.op("vector", lambda e, cg=cg: e.tensor_tensor_scan(
                            h_g.ap(), da_g.ap(), dbu_g.ap(), cg.ap(), ALU.mult, ALU.add),
                            reads=[da_g, dbu_g, carry], writes=[h_g], tag="scan")
                    if sec == 0:
                        tk.op("vector", lambda e, cg=cg: e.tensor_copy(cg.ap(), h_g.cview(SEC - 1, 1).ap()),
                              reads=[h_g], writes=[carry], tag="carrysave")
                    tk.op("vector", lambda e, tsl0=tsl0: e.tensor_mul(
                        ch_g.ap(), h_g.ap(), crep.cview(tsl0, SEC).ap()),
                        reads=[h_g, crep], writes=[ch_g], tag="chmul")
                    q = g % 8
                    for fq in range(NSC):
                        o = Tile(ctx.ps_wide, "psum", 8192, 16384, 0, 64, fq * 512, 512, F32)
                        tk.op("tensor", lambda e, o=o, q=q, fq=fq: e.matmul(
                            o.ap(), s8.cview(q * 64, 64).ap(), ch_g.cview(fq * 512, 512).ap(),
                            start=(q == 0), stop=(q == 7)),
                            reads=[s8, ch_g], writes=[o], tag="ssum")
                    if q == 7:
                        d0 = 8 * (g - 7)
                        src = Tile(ctx.ps_wide, "psum", 8192, 16384, 0, 64, 0, 2048, F32)
                        tk.op("vector", lambda e, src=src: e.tensor_copy(
                            ysev.pview(0, 64).ap(), src.ap()),
                            reads=[src], writes=[ysev], tag="ysev")
                        tk.op("sp", lambda e, d0=d0, tsl0=tsl0: e.dma_start(
                            d_ys.t[d0:d0 + 64, tsl0:tsl0 + SEC], ysev.pview(0, 64).ap()),
                            reads=[ysev], writes=[d_ys], dma=True, tag="ysout")
        A.release()

        if not mp(blk, 5):
            A.release()
            break
        A.mark()
        ys_j = A.alloc(T, BF16)
        uz_j = A.alloc(T, BF16)
        zz_j = A.alloc(T, BF16)
        sz_j = A.alloc(T, BF16)
        yo_j = A.alloc(T, BF16)
        for j in range(NJI):
            tk.op("sp", lambda e, j=j: e.dma_start(ys_j.ap(), d_ys.t[j * 128:(j + 1) * 128, :]),
                  reads=[d_ys], writes=[ys_j], dma=True, tag="ysin")
            tk.op("sp", lambda e, j=j: e.dma_start(uz_j.ap(), d_u.t[j * 128:(j + 1) * 128, :]),
                  reads=[d_u], writes=[uz_j], dma=True, tag="uin2")
            tk.op("sp", lambda e, j=j: e.dma_start(zz_j.ap(), d_z.t[j * 128:(j + 1) * 128, :]),
                  reads=[d_z], writes=[zz_j], dma=True, tag="zin")
            dcol = dpj.cview(j, 1)
            tk.op("vector", lambda e, dcol=dcol: e.scalar_tensor_tensor(
                yo_j.ap(), uz_j.ap(), dcol.ap(), ys_j.ap(), ALU.mult, ALU.add),
                reads=[uz_j, dpj, ys_j], writes=[yo_j], tag="uD")
            tk.op("scalar", lambda e: e.activation(sz_j.ap(), zz_j.ap(), AF.Silu),
                  reads=[zz_j], writes=[sz_j], tag="zsilu")
            tk.op("vector", lambda e: e.tensor_mul(yo_j.ap(), yo_j.ap(), sz_j.ap()),
                  reads=[yo_j, sz_j], writes=[yo_j], tag="ygate")
            tk.op("sp", lambda e, j=j: e.dma_start(d_yo.t[j * 128:(j + 1) * 128, :], yo_j.ap()),
                  reads=[yo_j], writes=[d_yo], dma=True, tag="yoout")
        A.release()

        if not mp(blk, 6):
            A.release()
            break
        A.mark()
        woT = A.alloc(NJI * 1024, BF16)
        tk.op("pool", lambda e, blk=blk: e.dma_start(
            woT.ap().rearrange("p (j d) -> p j d", j=NJI),
            W["mam_outT"].t[blk, :, :].rearrange("(j p) d -> p j d", p=128)),
            reads=[W["mam_outT"]], writes=[woT], dma=True, tag="woT")
        yosl_db = [A.alloc(NJI * 512, BF16) for _ in range(2)]
        ytsl_db = [A.alloc(512, BF16) for _ in range(2)]
        for f in range(8):
            yosl = yosl_db[f % 2]
            tk.op("sp", lambda e, yosl=yosl, f=f: e.dma_start(
                yosl.ap().rearrange("p (j c) -> p j c", j=NJI),
                d_yo.t[:, :].rearrange("(j p) t -> p j t", p=128)[:, :, f * 512:(f + 1) * 512]),
                reads=[d_yo], writes=[yosl], dma=True, tag="yosl")
            for op in range(NC):
                ps = ctx.ps("A" if op % 2 == 0 else "B", 0, 512)
                pairs = [(woT.cview(jc * 1024 + op * 128, 128), yosl.cview(jc * 512, 512))
                         for jc in range(NJI)]
                emit_mm_acc(ctx, ps, pairs, tag="outproj")
                if blk == 0:
                    dst = cur.cview(op * T + f * 512, 512)
                    tk.op("vector", lambda e, ps=ps, dst=dst: e.tensor_copy(dst.ap(), ps.ap()),
                          reads=[ps], writes=[dst], tag="curev")
                else:
                    ytsl = ytsl_db[op % 2]
                    tk.op("sp", lambda e, ytsl=ytsl, op=op, f=f: e.dma_start(
                        ytsl.ap(), d_yt.t[op * 128:(op + 1) * 128, f * 512:(f + 1) * 512]),
                        reads=[d_yt], writes=[ytsl], dma=True, tag="ytsl")
                    dst = cur.cview(op * T + f * 512, 512)
                    tk.op("vector", lambda e, ps=ps, ytsl=ytsl, dst=dst: e.tensor_add(
                        dst.ap(), ps.ap(), ytsl.ap()),
                        reads=[ps, ytsl], writes=[dst], tag="brev")
        A.release()
        A.release()

    xo_db = [A.alloc(1024, F32) for _ in range(2)]
    for i in range(NT):
        pt = ctx.ps("B" if i % 2 == 0 else "W", 0, 1024, BF16)
        for j in range(NC):
            src = cur.cview(j * T + i * 128, 128)
            dst = pt.cview(j * 128, 128)
            tk.op("tensor", lambda e, s=src, d=dst: e.transpose(d.ap(), s.ap(), ctx.ident.ap()),
                  reads=[src, ctx.ident], writes=[dst], tag="btr")
        xt = xt_db[i % 2]
        xo = xo_db[i % 2]
        row = i * 128
        tk.op("sp", lambda e, xt=xt, row=row: e.dma_start(
            xt.ap(), d_xin.t[row:row + 128, :]), reads=[d_xin], writes=[xt], dma=True, tag="xload2")
        tk.op("vector", lambda e, pt=pt, xt=xt, xo=xo: e.tensor_add(
            xo.ap(), pt.ap(), xt.ap()),
            reads=[pt, xt], writes=[xo], tag="resadd")
        tk.op("sp", lambda e, xo=xo, row=row: e.dma_start(
            d_xout.t[row:row + 128, :], xo.ap()), reads=[xo], writes=[d_xout], dma=True, tag="xstore")
    A.release()



def _prep_host(inputs):
    f32 = np.float32
    H = {}
    H["x_full"] = np.ascontiguousarray(inputs["L0_attn_x"], dtype=f32)
    H["l0f_wrT"] = np.ascontiguousarray(np.asarray(inputs["L0_ffn_W_r"], f32).T)
    H["l0f_wkT"] = np.ascontiguousarray(np.asarray(inputs["L0_ffn_W_k"], f32).T)
    H["l0f_wvT"] = np.ascontiguousarray(np.asarray(inputs["L0_ffn_W_v"], f32).T)
    H["l0f_mur"] = np.ascontiguousarray(np.asarray(inputs["L0_ffn_mu_r"], f32).reshape(8, 128).T)
    H["l0f_muk"] = np.ascontiguousarray(np.asarray(inputs["L0_ffn_mu_k"], f32).reshape(8, 128).T)
    H["moe_w1"] = np.ascontiguousarray(np.asarray(inputs["L1_ffn_W1"], f32))
    H["moe_b1pj"] = np.ascontiguousarray(
        np.asarray(inputs["L1_ffn_b1"], f32).reshape(8, 32, 128).transpose(0, 2, 1))
    H["moe_w2"] = np.ascontiguousarray(np.asarray(inputs["L1_ffn_W2"], f32))
    H["moe_b2"] = np.ascontiguousarray(np.asarray(inputs["L1_ffn_b2"], f32))
    H["b_wqT"] = np.ascontiguousarray(np.asarray(inputs["L0_attn_Wq"], f32).transpose(0, 2, 1))
    H["b_wkT"] = np.ascontiguousarray(np.asarray(inputs["L0_attn_Wk"], f32).transpose(0, 2, 1))
    H["b_bqpj"] = np.ascontiguousarray(np.asarray(inputs["L0_attn_bq"], f32).reshape(6, 8, 128).transpose(0, 2, 1))
    H["b_bkpj"] = np.ascontiguousarray(np.asarray(inputs["L0_attn_bk"], f32).reshape(6, 8, 128).transpose(0, 2, 1))
    for nm, key in (("b_ln1w", "L0_attn_ln1_w"), ("b_ln1b", "L0_attn_ln1_b"),
                    ("b_ln2w", "L0_attn_ln2_w"), ("b_ln2b", "L0_attn_ln2_b"),
                    ("b_ln3w", "L0_attn_ln3_w"), ("b_ln3b", "L0_attn_ln3_b"),
                    ("b_b2", "L0_attn_b2")):
        H[nm] = np.ascontiguousarray(np.asarray(inputs[key], f32))
    H["b_w1T"] = np.ascontiguousarray(np.asarray(inputs["L0_attn_W1"], f32).transpose(0, 2, 1))
    H["b_b1pj"] = np.ascontiguousarray(np.asarray(inputs["L0_attn_b1"], f32).reshape(6, 32, 128).transpose(0, 2, 1))
    H["b_w2T"] = np.ascontiguousarray(np.asarray(inputs["L0_attn_W2"], f32).transpose(0, 2, 1))
    pj = np.arange(8064)[None, :] - np.arange(128)[:, None]
    H["b_relt"] = np.ascontiguousarray(np.exp(-np.abs(3968.0 - pj) / 16.0).astype(f32))
    inw = np.asarray(inputs["L1_attn_in_proj_w"], f32)
    xpw = np.asarray(inputs["L1_attn_x_proj_w"], f32)
    dtw = np.asarray(inputs["L1_attn_dt_proj_w"], f32)
    onw = np.asarray(inputs["L1_attn_out_proj_w"], f32)
    cw = np.asarray(inputs["L1_attn_conv_w"], f32)[:, :, 0, :]
    cb = np.asarray(inputs["L1_attn_conv_b"], f32)
    dtb = np.asarray(inputs["L1_attn_dt_proj_b"], f32)
    alog = np.asarray(inputs["L1_attn_A_log"], f32)
    Dp = np.asarray(inputs["L1_attn_D"], f32)
    H["mam_inT"] = np.ascontiguousarray(inw.transpose(0, 2, 1))
    x1T = xpw[:, 0:64].transpose(0, 2, 1)
    x2T = xpw[:, 64:96].transpose(0, 2, 1)
    H["mam_x1T"] = np.ascontiguousarray(x1T.reshape(2, 16, 128, 64).transpose(0, 2, 1, 3).reshape(2, 128, 1024))
    H["mam_x2T"] = np.ascontiguousarray(x2T.reshape(2, 16, 128, 32).transpose(0, 2, 1, 3).reshape(2, 128, 512))
    H["mam_dtT"] = np.ascontiguousarray(dtw.transpose(0, 2, 1))
    H["mam_outT"] = np.ascontiguousarray(onw.transpose(0, 2, 1))
    H["mam_apj"] = np.ascontiguousarray(alog.reshape(2, 32768).reshape(2, 256, 128).transpose(0, 2, 1))
    H["mam_dpj"] = np.ascontiguousarray(Dp.reshape(2, 16, 128).transpose(0, 2, 1))
    H["mam_cwpj"] = np.ascontiguousarray(cw.reshape(2, 16, 128, 4).transpose(0, 2, 1, 3).reshape(2, 128, 64))
    H["mam_cbpj"] = np.ascontiguousarray(cb.reshape(2, 16, 128).transpose(0, 2, 1))
    H["mam_dtbpj"] = np.ascontiguousarray(dtb.reshape(2, 16, 128).transpose(0, 2, 1))
    cc = np.arange(128)
    pp = np.arange(128)
    H["mam_rrep"] = np.ascontiguousarray(np.stack(
        [(cc[:, None] == (8 * g + pp[None, :] // 16)).astype(f32) for g in range(16)]))
    p64 = np.arange(64)
    H["mam_s8"] = np.ascontiguousarray(np.stack(
        [(p64[None, :] == (8 * q + cc[:, None] // 16)).astype(f32) for q in range(8)]))
    c32 = np.arange(32)
    H["mam_r16b"] = np.ascontiguousarray((c32[:, None] == (pp[None, :] % 16)).astype(f32))
    H["mam_r16c"] = np.ascontiguousarray((c32[:, None] == (16 + pp[None, :] % 16)).astype(f32))
    return H


STAGE_DRAMS = {
    "ffn0": {
        "l0f_wrT": ((C, C), F32, "ExternalInput"),
        "l0f_wkT": ((C, C), F32, "ExternalInput"),
        "l0f_wvT": ((C, C), F32, "ExternalInput"),
        "l0f_mur": ((128, 8), F32, "ExternalInput"),
        "l0f_muk": ((128, 8), F32, "ExternalInput"),
    },
    "attn": {
        "b_wqT": ((6, C, C), F32, "ExternalInput"),
        "b_wkT": ((6, C, C), F32, "ExternalInput"),
        "b_bqpj": ((6, 128, 8), F32, "ExternalInput"),
        "b_bkpj": ((6, 128, 8), F32, "ExternalInput"),
        "b_ln1w": ((6, C), F32, "ExternalInput"),
        "b_ln1b": ((6, C), F32, "ExternalInput"),
        "b_ln2w": ((6, C), F32, "ExternalInput"),
        "b_ln2b": ((6, C), F32, "ExternalInput"),
        "b_ln3w": ((6, C), F32, "ExternalInput"),
        "b_ln3b": ((6, C), F32, "ExternalInput"),
        "b_w1T": ((6, C, 4096), F32, "ExternalInput"),
        "b_b1pj": ((6, 128, 32), F32, "ExternalInput"),
        "b_w2T": ((6, 4096, C), F32, "ExternalInput"),
        "b_b2": ((6, C), F32, "ExternalInput"),
        "b_relt": ((128, 8064), F32, "ExternalInput"),
        "b_h": ((T, C), F32, "Internal"),
        "b_h2": ((T, C), F32, "Internal"),
        "b_qt": ((C, T), BF16, "Internal"),
        "b_pt": ((32, 128, T), BF16, "Internal"),
    },
    "mamba": {
        "mam_inT": ((2, C, 4096), F32, "ExternalInput"),
        "mam_x1T": ((2, 128, 1024), F32, "ExternalInput"),
        "mam_x2T": ((2, 128, 512), F32, "ExternalInput"),
        "mam_dtT": ((2, 64, DI), F32, "ExternalInput"),
        "mam_outT": ((2, DI, C), F32, "ExternalInput"),
        "mam_apj": ((2, 128, 256), F32, "ExternalInput"),
        "mam_dpj": ((2, 128, 16), F32, "ExternalInput"),
        "mam_cwpj": ((2, 128, 64), F32, "ExternalInput"),
        "mam_cbpj": ((2, 128, 16), F32, "ExternalInput"),
        "mam_dtbpj": ((2, 128, 16), F32, "ExternalInput"),
        "mam_rrep": ((16, 128, 128), F32, "ExternalInput"),
        "mam_s8": ((8, 128, 64), F32, "ExternalInput"),
        "mam_r16b": ((32, 128), F32, "ExternalInput"),
        "mam_r16c": ((32, 128), F32, "ExternalInput"),
        "mam_yt": ((C, T), BF16, "Internal"),
        "mam_u": ((DI, T), BF16, "Internal"),
        "mam_z": ((DI, T), BF16, "Internal"),
        "mam_dl": ((DI, T), BF16, "Internal"),
        "mam_ys": ((DI, T), BF16, "Internal"),
        "mam_yo": ((DI, T), BF16, "Internal"),
    },
    "moe": {
        "moe_w1": ((8, C, 4096), F32, "ExternalInput"),
        "moe_b1pj": ((8, 128, 32), F32, "ExternalInput"),
        "moe_w2": ((8, 4096, C), F32, "ExternalInput"),
        "moe_b2": ((8, C), F32, "ExternalInput"),
        "moe_yt": ((128, NC * T), BF16, "Internal"),
        "moe_wt": ((256, 128), BF16, "Internal"),
        "moe_acc": ((128, NC * T), F32, "Internal"),
    },
}

STAGE_KIND = {"A": "ffn0", "B": "attn", "C": "ffn0", "D": "moe", "E": "mamba", "F": "moe"}


def _emit_stage(ctx, stage, d, d_in, d_out):
    kind = STAGE_KIND[stage]
    if kind == "ffn0":
        Wf = {"wrT": d["l0f_wrT"], "wkT": d["l0f_wkT"], "wvT": d["l0f_wvT"],
              "mur_pj": d["l0f_mur"], "muk_pj": d["l0f_muk"]}
        emit_stage_ffn0(ctx, d_in, d_out, Wf, alpha=0.5)
    elif kind == "moe":
        Wm = {"moe_w1": d["moe_w1"], "moe_b1pj": d["moe_b1pj"], "moe_w2": d["moe_w2"],
              "moe_b2": d["moe_b2"], "d_yt": d["moe_yt"], "d_wt": d["moe_wt"],
              "d_acc": d["moe_acc"]}
        emit_stage_moe(ctx, d_in, d_out, Wm, alpha=0.5)
    elif kind == "attn":
        Wa = {k: d[k] for k in STAGE_DRAMS["attn"]}
        emit_stage_attn(ctx, d_in, d_out, Wa)
    elif kind == "mamba":
        Wb = {k: d[k] for k in STAGE_DRAMS["mamba"]}
        emit_stage_mamba(ctx, d_in, d_out, Wb)
    else:
        raise ValueError(stage)


def _build_chain(stages):
    nc = bass.Bass("TRN2", target_bir_lowering=False)
    d = {}

    def dram(name, shape, dtype, kind):
        if name in d:
            return d[name]
        t = nc.dram_tensor(name, list(shape), dtype, kind=kind)
        d[name] = DramT(t, "d_" + name, shape, dtype)
        return d[name]

    dram("x", (T, C), F32, "ExternalInput")
    dram("y", (T, C), F32, "ExternalOutput")
    kinds = {STAGE_KIND[s] for s in stages}
    for k in kinds:
        for name, (shape, dtype, dkind) in STAGE_DRAMS[k].items():
            dram(name, shape, dtype, dkind)
    if len(stages) > 1:
        dram("xp0", (T, C), F32, "Internal")
        dram("xp1", (T, C), F32, "Internal")

    tk = TK(nc)
    arena = Arena(nc, "arena", 206 * 1024)
    ctx = Ctx(nc, tk, arena)
    cur = d["x"]
    for si, s in enumerate(stages):
        last = si == len(stages) - 1
        nxt = d["y"] if last else d["xp%d" % (si % 2)]
        _emit_stage(ctx, s, d, cur, nxt)
        cur = nxt
    tk.barrier_read_all_dmas()
    tk.finalize()
    nc.finalize()
    return nc


CHAIN = ["A", "B", "C", "D", "E", "F"]
GROUPS = [["A", "B", "C", "D", "E", "F"]]


def kernel(**inputs):
    H = _prep_host(inputs)
    x = H["x_full"]
    groups = GROUPS if GROUPS is not None else [CHAIN]
    wkeys = set()
    for g in groups:
        for s in g:
            for name, (_, _, k) in STAGE_DRAMS[STAGE_KIND[s]].items():
                if k == "ExternalInput":
                    wkeys.add(name)
    cur = [np.ascontiguousarray(x[c]) for c in range(8)]
    for g in groups:
        nc = _build_chain(g)
        gw = set()
        for s in g:
            for name, (_, _, k) in STAGE_DRAMS[STAGE_KIND[s]].items():
                if k == "ExternalInput":
                    gw.add(name)
        in_maps = []
        for c in range(8):
            m = {"x": cur[c]}
            for name in gw:
                m[name] = H[name]
            in_maps.append(m)
        res = run_bass_kernel_spmd(nc, in_maps, core_ids=list(range(8)))
        cur = [np.ascontiguousarray(r["y"]) for r in res.results]
    return np.stack(cur, axis=0)



# version 05 (final)
# speedup vs baseline: 1.0336x; 1.0336x over previous
import math
import numpy as np
import concourse.bass as bass
import concourse.mybir as mybir
from concourse.bass_utils import run_bass_kernel_spmd

F32 = mybir.dt.float32
BF16 = mybir.dt.bfloat16
I32 = mybir.dt.int32
AF = mybir.ActivationFunctionType
ALU = mybir.AluOpType
AX = mybir.AxisListType

PSUM_BANK_BYTES = 2048


ENGINES = ("tensor", "vector", "scalar", "pool", "sp")
DMA_QUEUES = {"sp": 0, "scalar": 1, "pool": 2}


class Tile:

    __slots__ = ("t", "pool", "lo", "hi", "part0", "parts", "coff", "cols", "dtype")

    def __init__(self, t, pool, lo, hi, part0, parts, coff, cols, dtype):
        self.t = t
        self.pool = pool
        self.lo = lo
        self.hi = hi
        self.part0 = part0
        self.parts = parts
        self.coff = coff
        self.cols = cols
        self.dtype = dtype

    def ap(self):
        try:
            same = self.t.dtype == self.dtype
        except AttributeError:
            same = True
        if same:
            return self.t[self.part0:self.part0 + self.parts,
                          self.coff:self.coff + self.cols]
        return self.t[self.part0:self.part0 + self.parts, :].bitcast(self.dtype)[
            :, self.coff:self.coff + self.cols]

    def cview(self, c0, n):
        assert 0 <= c0 and c0 + n <= self.cols, (c0, n, self.cols)
        if self.pool == "psum":
            return Tile(self.t, self.pool, self.lo, self.hi,
                        self.part0, self.parts, self.coff + c0, n, self.dtype)
        esz = _esize(self.dtype)
        return Tile(self.t, self.pool, self.lo + c0 * esz, self.lo + (c0 + n) * esz,
                    self.part0, self.parts, self.coff + c0, n, self.dtype)

    def pview(self, p0, np_):
        assert 0 <= p0 and p0 + np_ <= self.parts
        return Tile(self.t, self.pool, self.lo, self.hi,
                    self.part0 + p0, np_, self.coff, self.cols, self.dtype)

    @property
    def shape(self):
        return (self.parts, self.cols)


def _esize(dt):
    return {F32: 4, BF16: 2, I32: 4, mybir.dt.float16: 2, mybir.dt.uint8: 1}[dt]


class DramT:

    __slots__ = ("t", "pool", "shape", "dtype", "esz")

    def __init__(self, t, pool, shape, dtype):
        self.t = t
        self.pool = pool
        self.shape = shape
        self.dtype = dtype
        self.esz = _esize(dtype)

    def range_for(self, index_expr_lo, index_expr_hi):
        return (index_expr_lo * self.esz, index_expr_hi * self.esz)


class _Op:
    __slots__ = ("eng", "emit", "waits", "inc", "dma_q", "dma_slot", "tag", "seq")

    def __init__(self, eng, emit, waits, dma_q, tag, seq):
        self.eng = eng
        self.emit = emit
        self.waits = waits
        self.inc = False
        self.dma_q = dma_q
        self.dma_slot = None
        self.tag = tag
        self.seq = seq


DMA_RING = 24


class TK:
    def __init__(self, nc):
        self.nc = nc
        self.ops = []
        self.pools = {}
        self.dma_issue = {q: 0 for q in DMA_QUEUES.values()}
        self.dma_use = {}
        self.wm = {e: {} for e in ENGINES}
        self._eng_seq = {e: 0 for e in ENGINES}
        self._op_by_ev = {}
        self.frozen = False

    def _resolve(self, eng, reads, writes):
        need = {}

        def add_ev(ev):
            if ev is None:
                return
            kind = ev[0]
            if kind == "e":
                _, peng, pseq = ev
                if peng == eng == "tensor":
                    return
                if peng == eng and pseq is not None:
                    pass
                key = ("e", peng)
                cur = need.get(key, -1)
                if pseq > cur:
                    need[key] = pseq
            else:
                _, q, slot, n = ev
                key = ("d", q, slot)
                cur = need.get(key, -1)
                if n > cur:
                    need[key] = n

        for (pool, lo, hi) in reads:
            for seg in self._overlap(pool, lo, hi):
                add_ev(seg[2])
        for (pool, lo, hi) in writes:
            for seg in self._overlap(pool, lo, hi):
                add_ev(seg[2])
                for rev in seg[3]:
                    add_ev(rev)

        waits = []
        wmk = self.wm[eng]
        for key, val in need.items():
            if key[0] == "e" and key[1] == eng:
                pass
            if wmk.get(key, -1) >= val:
                continue
            wmk[key] = val
            waits.append((key, val))
        return waits

    def _overlap(self, pool, lo, hi):
        segs = self.pools.get(pool)
        if not segs:
            return ()
        return [s for s in segs if s[0] < hi and s[1] > lo]

    def _commit_write(self, pool, lo, hi, ev):
        segs = self.pools.setdefault(pool, [])
        out = []
        for s in segs:
            if s[0] < hi and s[1] > lo:
                if s[0] < lo:
                    out.append([s[0], lo, s[2], s[3]])
                if s[1] > hi:
                    out.append([hi, s[1], s[2], s[3]])
            else:
                out.append(s)
        out.append([lo, hi, ev, []])
        self.pools[pool] = out

    def _commit_read(self, pool, lo, hi, ev):
        segs = self.pools.get(pool)
        if not segs:
            self.pools[pool] = [[lo, hi, None, [ev]]]
            return
        covered = False
        for s in segs:
            if s[0] < hi and s[1] > lo:
                s[3].append(ev)
                covered = True
        if not covered:
            segs.append([lo, hi, None, [ev]])

    def op(self, eng, emit, reads=(), writes=(), dma=False, tag=""):
        assert not self.frozen

        def norm(x):
            if isinstance(x, Tile):
                return (x.pool, x.lo, x.hi)
            if isinstance(x, DramT):
                return (x.pool, 0, 1 << 60)
            return x

        reads = [norm(x) for x in reads]
        writes = [norm(x) for x in writes]
        psr = [r for r in reads if r[0] == "psum"]
        if psr:
            reads = [r for r in reads if r[0] != "psum"]
            writes = writes + psr
        waits = self._resolve(eng, reads, writes)
        seq = self._eng_seq[eng]
        self._eng_seq[eng] += 1
        dq = DMA_QUEUES[eng] if dma else None
        slot = None
        if dq is not None:
            i = self.dma_issue[dq]
            self.dma_issue[dq] = i + 1
            slot = i % DMA_RING
            use = i // DMA_RING + 1
            if use > 1:
                key = ("d", dq, slot)
                if self.wm[eng].get(key, -1) < use - 1:
                    self.wm[eng][key] = use - 1
                    waits.append((key, use - 1))
            self.dma_use[(dq, slot)] = use
        o = _Op(eng, emit, waits, dq, tag, seq)
        o.dma_slot = slot
        self.ops.append(o)
        self._op_by_ev[("e", eng, seq)] = o
        for (key, val) in waits:
            if key[0] == "e":
                self._op_by_ev[("e", key[1], val)].inc = True
        if dq is not None:
            ev = ("d", dq, slot, self.dma_use[(dq, slot)])
        else:
            ev = ("e", eng, seq)
        for (pool, lo, hi) in writes:
            self._commit_write(pool, lo, hi, ev)
        for (pool, lo, hi) in reads:
            self._commit_read(pool, lo, hi, ev)
        return ev

    def barrier_read_all_dmas(self):
        eng = "sp"
        waits = []
        for (q, slot), n in self.dma_use.items():
            key = ("d", q, slot)
            if n > 0 and self.wm[eng].get(key, -1) < n:
                waits.append((key, n))
                self.wm[eng][key] = n
        for e in ENGINES:
            if e == eng:
                continue
            seq = self._eng_seq[e] - 1
            while seq >= 0 and self._op_by_ev[("e", e, seq)].dma_q is not None:
                seq -= 1
            if seq >= 0 and self.wm[eng].get(("e", e), -1) < seq:
                o = self._op_by_ev[("e", e, seq)]
                o.inc = True
                waits.append((("e", e), seq))
                self.wm[eng][("e", e)] = seq
        o = _Op(eng, None, waits, None, "final_barrier", self._eng_seq[eng])
        self._eng_seq[eng] += 1
        self.ops.append(o)

    def finalize(self):
        self.frozen = True
        nc = self.nc
        cum = {}
        for e in ENGINES:
            cnt = 0
            arr = []
            for o in self.ops:
                if o.eng != e:
                    continue
                if o.inc:
                    cnt += 1
                arr.append(cnt)
            cum[e] = arr
        self._sems = {}
        for e in ENGINES:
            h = nc.semaphore(f"sem_{e}").__enter__()
            self._sems[("e", e)] = h
        for q in set(DMA_QUEUES.values()):
            for s in range(DMA_RING):
                if self.dma_use.get((q, s)):
                    h = nc.semaphore(f"sem_dq{q}_{s}").__enter__()
                    self._sems[("d", q, s)] = h

        eng_handles = {
            "tensor": nc.tensor, "vector": nc.vector, "scalar": nc.scalar,
            "pool": nc.gpsimd, "sp": nc.sync,
        }
        per_eng_ops = {e: [o for o in self.ops if o.eng == e] for e in ENGINES}

        with nc.Block() as block:
            for ename in ENGINES:
                ops = per_eng_ops[ename]
                if not ops:
                    continue
                eh = eng_handles[ename]

                def emit_stream(_eng, ops=ops, eh=eh, ename=ename):
                    for o in ops:
                        for (key, val) in o.waits:
                            if key[0] == "e":
                                v = cum[key[1]][val]
                                eh.wait_ge(self._sems[key], v)
                            else:
                                eh.wait_ge(self._sems[key], 16 * val)
                        if o.emit is None:
                            continue
                        inst = o.emit(eh)
                        if o.dma_q is not None:
                            inst.then_inc(self._sems[("d", o.dma_q, o.dma_slot)], 16)
                            if o.inc:
                                raise RuntimeError("dma op with engine inc")
                        elif o.inc:
                            inst.then_inc(self._sems[("e", ename)], 1)

                getattr(block, {"tensor": "tensor", "vector": "vector",
                                "scalar": "scalar", "pool": "gpsimd",
                                "sp": "sync"}[ename])(emit_stream)
        return



class Arena:
    def __init__(self, nc, name, nbytes):
        self.nbytes = nbytes
        self.name = name
        self.t8 = nc.alloc_sbuf_tensor(name, [128, nbytes], mybir.dt.uint8)
        self.pool = name
        self.free_ptr = 0
        self.stack = []

    peak = 0

    def alloc(self, cols, dtype, parts=128, align=64):
        esz = _esize(dtype)
        nb = cols * esz
        lo = (self.free_ptr + align - 1) // align * align
        hi = lo + nb
        assert hi <= self.nbytes, f"{self.name} arena overflow: {hi} > {self.nbytes}"
        self.free_ptr = hi
        Arena.peak = max(Arena.peak, hi)
        return ArenaTile(self, lo, parts, cols, dtype)

    def mark(self):
        self.stack.append(self.free_ptr)

    def release(self):
        self.free_ptr = self.stack.pop()


class ArenaTile(Tile):
    def __init__(self, arena, lo, parts, cols, dtype):
        esz = _esize(dtype)
        assert lo % esz == 0
        Tile.__init__(self, arena.t8, arena.pool, lo, lo + cols * esz,
                      0, parts, lo // esz, cols, dtype)


T = 4096
C = 1024
NT = T // 128
NC = C // 128
EPS = 1e-5


class Ctx:
    def __init__(self, nc, tk, arena):
        self.nc = nc
        self.tk = tk
        self.A = arena
        self.ps_mmA = nc.alloc_psum_tensor("ps_mmA", [128, 1024], F32)
        self.ps_mmB = nc.alloc_psum_tensor("ps_mmB", [128, 1024], F32)
        self.ps_wide = nc.alloc_psum_tensor("ps_wide", [128, 2048], F32)
        self._ps = {"A": (self.ps_mmA, 0), "B": (self.ps_mmB, 2), "W": (self.ps_wide, 4)}
        self.sq_dump = arena.alloc(1024, BF16)
        self.stat = arena.alloc(64, F32)
        self.ident = arena.alloc(128, BF16)
        self.eqm = arena.alloc(128, F32)
        self._made_ident = False
        self._consts = {}
        self._const_area = arena.alloc(32, F32)
        self._const_n = 0

    def constf(self, val):
        val = float(val)
        if val in self._consts:
            return self._consts[val]
        c = self._const_area.cview(self._const_n, 1)
        self._const_n += 1
        self.tk.op("pool", lambda e: e.memset(c.ap(), val), writes=[c], tag="const")
        self.nc.const_aps.register(F32, val, c.ap())
        self._consts[val] = c
        return c

    _PS_SPAN = {"A": (0, 4096), "B": (4096, 8192), "W": (8192, 16384)}

    def ps(self, region, col0, ncols, dtype=F32):
        t, _ = self._ps[region]
        lo, hi = Ctx._PS_SPAN[region]
        return Tile(t, "psum", lo, hi, 0, 128, col0, ncols, dtype)

    def make_ident(self):
        if self._made_ident:
            return
        self._made_ident = True
        tk, A = self.tk, self.A
        idx = A.alloc(128, F32)
        eqm = self.eqm
        tk.op("pool", lambda e: e.iota(idx.ap(), [[1, 128]], base=0, channel_multiplier=-1,
                                       allow_small_or_imprecise_dtypes=True), writes=[idx], tag="ident_iota")
        tk.op("vector", lambda e: e.tensor_single_scalar(eqm.ap(), idx.ap(), 0.0, ALU.is_equal),
              reads=[idx], writes=[eqm], tag="ident_eq")
        tk.op("vector", lambda e: e.tensor_copy(self.ident.ap(), eqm.ap()),
              reads=[eqm], writes=[self.ident], tag="ident_cp")



def bcast_row(ctx, dst_tile, dram_t, row_ap):
    ctx.tk.op("sp", lambda e: e.dma_start(dst_tile.ap(), row_ap.broadcast_to((128, dst_tile.cols))),
              reads=[dram_t], writes=[dst_tile], dma=True, tag="bcast")


def emit_ln(ctx, xt, out_tile, stat_col0, affine=None):
    tk = ctx.tk
    st = ctx.stat
    s_sum = st.cview(stat_col0 + 0, 1)
    s_ssq = st.cview(stat_col0 + 1, 1)
    s_mean = st.cview(stat_col0 + 2, 1)
    s_var = st.cview(stat_col0 + 3, 1)
    s_rstd = st.cview(stat_col0 + 4, 1)
    n = xt.cols
    tk.op("vector", lambda e: e.reduce_sum(s_sum.ap(), xt.ap(), AX.X), reads=[xt], writes=[s_sum], tag="ln_sum")
    tk.op("scalar", lambda e: e.activation(ctx.sq_dump.cview(0, n).ap(), xt.ap(), AF.Square,
                                           accum_out=s_ssq.ap()),
          reads=[xt], writes=[ctx.sq_dump, s_ssq], tag="ln_ssq")
    tk.op("vector", lambda e: e.tensor_single_scalar(s_mean.ap(), s_sum.ap(), 1.0 / n, ALU.mult),
          reads=[s_sum], writes=[s_mean], tag="ln_mean")
    tk.op("vector", lambda e: e.tensor_mul(s_var.ap(), s_mean.ap(), s_mean.ap()),
          reads=[s_mean], writes=[s_var], tag="ln_msq")
    tk.op("vector", lambda e: e.scalar_tensor_tensor(s_var.ap(), s_ssq.ap(), 1.0 / n, s_var.ap(),
                                                     ALU.mult, ALU.subtract),
          reads=[s_ssq, s_var], writes=[s_var], tag="ln_var")
    ceps = ctx.constf(EPS)
    tk.op("scalar", lambda e: e.activation(s_rstd.ap(), s_var.ap(), AF.Ln, bias=EPS),
          reads=[s_var, ceps], writes=[s_rstd], tag="ln_ln")
    tk.op("scalar", lambda e: e.activation(s_rstd.ap(), s_rstd.ap(), AF.Exp, scale=-0.5),
          reads=[s_rstd], writes=[s_rstd], tag="ln_exp")
    tk.op("vector", lambda e: e.tensor_scalar(out_tile.ap(), xt.ap(), s_mean.ap(), s_rstd.ap(),
                                              ALU.subtract, ALU.mult),
          reads=[xt, s_mean, s_rstd], writes=[out_tile], tag="ln_apply")
    if affine is not None:
        w_t, b_t = affine
        tk.op("vector", lambda e: e.tensor_mul(out_tile.ap(), out_tile.ap(), w_t.ap()),
              reads=[out_tile, w_t], writes=[out_tile], tag="ln_w")
        tk.op("vector", lambda e: e.tensor_add(out_tile.ap(), out_tile.ap(), b_t.ap()),
              reads=[out_tile, b_t], writes=[out_tile], tag="ln_b")


def emit_transpose_rt(ctx, dstT, src_tiles, gstride, psbank_region="B"):
    tk = ctx.tk
    ctx.make_ident()
    nt = len(src_tiles)
    for i in range(nt):
        pt = ctx.ps(psbank_region, (i % 2) * 1024, 1024, BF16)
        for j in range(NC):
            src = src_tiles[i].cview(j * 128, 128)
            dst = pt.cview(j * 128, 128)
            tk.op("tensor", lambda e, s=src, d=dst: e.transpose(d.ap(), s.ap(), ctx.ident.ap()),
                  reads=[src, ctx.ident], writes=[dst], tag="tr")

        def emit_evict(e, pt=pt, i=i):
            src_ap = pt.ap().rearrange("p (j c) -> p j c", j=NC)
            dst_ap = dstT.ap().rearrange("p (j g) -> p j g", j=NC)[:, :, i * 128:(i + 1) * 128]
            return e.tensor_copy(dst_ap, src_ap)

        tk.op("vector", emit_evict, reads=[pt], writes=[dstT], tag="tr_ev")


def emit_mm_acc(ctx, psum_tile, pairs, tag="mm"):
    tk = ctx.tk
    nk = len(pairs)
    for k, (lt, rt) in enumerate(pairs):
        tk.op("tensor", lambda e, lt=lt, rt=rt, k=k: e.matmul(
            psum_tile.ap(), lt.ap(), rt.ap(), start=(k == 0), stop=(k == nk - 1)),
            reads=[lt, rt], writes=[psum_tile], tag=tag)


def emit_transpose_tile(ctx, dstT, gstride, src_tile, i, psbank_region="B"):
    tk = ctx.tk
    ctx.make_ident()
    pt = ctx.ps("B" if i % 2 == 0 else "W", 0, 1024, BF16)
    for j in range(NC):
        src = src_tile.cview(j * 128, 128)
        dst = pt.cview(j * 128, 128)
        tk.op("tensor", lambda e, s=src, d=dst: e.transpose(d.ap(), s.ap(), ctx.ident.ap()),
              reads=[src, ctx.ident], writes=[dst], tag="tr")

    def emit_evict(e, pt=pt, i=i):
        src_ap = pt.ap().rearrange("p (j c) -> p j c", j=NC)
        dst_ap = dstT.ap().rearrange("p (j g) -> p j g", j=NC)[:, :, i * 128:(i + 1) * 128]
        return e.tensor_copy(dst_ap, src_ap)

    tk.op("vector", emit_evict, reads=[pt], writes=[dstT], tag="tr_ev")



DEBUG_PHASE = 99


def emit_stage_ffn0(ctx, d_xin, d_xout, W, alpha=0.5):
    tk, A = ctx.tk, ctx.A
    TH = T // 2
    NTH = TH // 128
    NF = TH // 512

    A.mark()
    wbufs = {}
    for name in ("wrT", "wkT", "wvT"):
        wt = A.alloc(NC * 1024, BF16)
        dram = W[name]
        tk.op("pool", lambda e, wt=wt, dram=dram: e.dma_start(
            wt.ap().rearrange("p (j d) -> p j d", j=NC),
            dram.t[:, :].rearrange("(j p) d -> p j d", p=128)),
            reads=[dram], writes=[wt], dma=True, tag="wload")
        wbufs[name] = wt
    mur = A.alloc(8, F32)
    muk = A.alloc(8, F32)
    tk.op("sp", lambda e: e.dma_start(mur.ap(), W["mur_pj"].t[:, :]), reads=[W["mur_pj"]], writes=[mur], dma=True)
    tk.op("sp", lambda e: e.dma_start(muk.ap(), W["muk_pj"].t[:, :]), reads=[W["muk_pj"]], writes=[muk], dma=True)

    S = [A.alloc(NC * TH, BF16) for _ in range(4)]
    pcol = A.alloc(8, BF16)
    pcol_z = A.alloc(8, BF16)
    tk.op("pool", lambda e: e.memset(pcol_z.ap(), 0.0), writes=[pcol_z], tag="pz")
    ttmp = A.alloc(2048, BF16)
    xt_db = [A.alloc(1024, F32) for _ in range(2)]
    ybf_db = [A.alloc(1024, BF16) for _ in range(2)]
    kq_db = [A.alloc(512, BF16) for _ in range(2)]
    xo_db = [A.alloc(1024, F32) for _ in range(2)]

    for half in range(2):
        t0 = half * TH
        yT, xrT, xkT, srT = S[0], S[1], S[2], S[3]
        for i in range(NTH):
            xt = xt_db[i % 2]
            ybf = ybf_db[i % 2]
            row = t0 + i * 128
            tk.op("sp", lambda e, xt=xt, row=row: e.dma_start(
                xt.ap(), d_xin.t[row:row + 128, :]), reads=[d_xin], writes=[xt], dma=True, tag="xload")
            emit_ln(ctx, xt, ybf, (i % 2) * 6)
            emit_transpose_tile(ctx, yT, TH, ybf, i)
        if DEBUG_PHASE <= 1:
            continue
        pc = pcol_z if half == 0 else pcol
        for j in range(NC):
            yj = yT.cview(j * TH, TH)
            xrj = xrT.cview(j * TH, TH)
            xkj = xkT.cview(j * TH, TH)
            murj = mur.cview(j, 1)
            mukj = muk.cview(j, 1)
            pcj = pc.cview(j, 1)
            tk.op("vector", lambda e, yj=yj: e.tensor_sub(ttmp.cview(1, TH - 1).ap(),
                                                          yj.cview(1, TH - 1).ap(), yj.cview(0, TH - 1).ap()),
                  reads=[yj], writes=[ttmp], tag="mixd")
            tk.op("vector", lambda e, yj=yj, pcj=pcj: e.tensor_sub(ttmp.cview(0, 1).ap(),
                                                                   yj.cview(0, 1).ap(), pcj.ap()),
                  reads=[yj, pc], writes=[ttmp], tag="mixd0")
            for dst, mcol in ((xrj, murj), (xkj, mukj)):
                tk.op("vector", lambda e, dst=dst, mcol=mcol, yj=yj: e.scalar_tensor_tensor(
                    dst.cview(1, TH - 1).ap(), ttmp.cview(1, TH - 1).ap(), mcol.ap(),
                    yj.cview(0, TH - 1).ap(), ALU.mult, ALU.add),
                    reads=[ttmp, mcol, yj], writes=[dst], tag="mix")
                tk.op("vector", lambda e, dst=dst, mcol=mcol, pcj=pcj: e.scalar_tensor_tensor(
                    dst.cview(0, 1).ap(), ttmp.cview(0, 1).ap(), mcol.ap(), pcj.ap(),
                    ALU.mult, ALU.add),
                    reads=[ttmp, mcol, pc], writes=[dst], tag="mix0")
        if half == 0:
            def sv(e):
                src = yT.ap().rearrange("p (j g) -> p j g", j=NC)[:, :, TH - 1:TH]
                return e.tensor_copy(pcol.ap().rearrange("p (j o) -> p j o", j=NC), src)
            tk.op("vector", sv, reads=[yT], writes=[pcol], tag="pcol")

        if DEBUG_PHASE <= 2:
            continue

        def mm_stage(wname, rhsT, evict):
            wt = wbufs[wname]
            for dp in range(NC):
                for f in range(NF):
                    par = (dp * NF + f) % 2
                    ps = ctx.ps("A" if par == 0 else "B", 0, 512)
                    pairs = [(wt.cview(jc * 1024 + dp * 128, 128), rhsT.cview(jc * TH + f * 512, 512))
                             for jc in range(NC)]
                    emit_mm_acc(ctx, ps, pairs, tag=wname)
                    evict(ps, dp, f)

        def ev_sig(ps, dp, f):
            dst = srT.cview(dp * TH + f * 512, 512)
            tk.op("scalar", lambda e, ps=ps, dst=dst: e.activation(dst.ap(), ps.ap(), AF.Sigmoid),
                  reads=[ps], writes=[dst], tag="sig")
        mm_stage("wrT", xrT, ev_sig)
        if DEBUG_PHASE <= 3:
            continue
        kact = yT
        def ev_k(ps, dp, f):
            kq = kq_db[(dp * NF + f) % 2]
            dst = kact.cview(dp * TH + f * 512, 512)
            tk.op("vector", lambda e, ps=ps, kq=kq: e.tensor_single_scalar(kq.ap(), ps.ap(), 0.0, ALU.max),
                  reads=[ps], writes=[kq], tag="relu")
            tk.op("scalar", lambda e, kq=kq, dst=dst: e.activation(dst.ap(), kq.ap(), AF.Square),
                  reads=[kq], writes=[dst], tag="sq")
        mm_stage("wkT", xkT, ev_k)
        if DEBUG_PHASE <= 4:
            continue
        outTb = xrT
        def ev_v(ps, dp, f):
            sl = dp * TH + f * 512
            tk.op("vector", lambda e, ps=ps, sl=sl: e.tensor_mul(
                outTb.cview(sl, 512).ap(), ps.ap(), srT.cview(sl, 512).ap()),
                reads=[ps, srT], writes=[outTb], tag="gate")
        mm_stage("wvT", kact, ev_v)
        if DEBUG_PHASE <= 5:
            continue

        for i in range(NTH):
            pt = ctx.ps("B" if i % 2 == 0 else "W", 0, 1024, BF16)
            for j in range(NC):
                src = outTb.cview(j * TH + i * 128, 128)
                dst = pt.cview(j * 128, 128)
                tk.op("tensor", lambda e, s=src, d=dst: e.transpose(d.ap(), s.ap(), ctx.ident.ap()),
                      reads=[src, ctx.ident], writes=[dst], tag="btr")
            xt = xt_db[i % 2]
            xo = xo_db[i % 2]
            row = t0 + i * 128
            tk.op("sp", lambda e, xt=xt, row=row: e.dma_start(
                xt.ap(), d_xin.t[row:row + 128, :]), reads=[d_xin], writes=[xt], dma=True, tag="xload2")
            tk.op("vector", lambda e, pt=pt, xt=xt, xo=xo: e.scalar_tensor_tensor(
                xo.ap(), pt.ap(), float(alpha), xt.ap(), ALU.mult, ALU.add),
                reads=[pt, xt], writes=[xo], tag="resadd")
            tk.op("sp", lambda e, xo=xo, row=row: e.dma_start(
                d_xout.t[row:row + 128, :], xo.ap()), reads=[xo], writes=[d_xout], dma=True, tag="xstore")
    A.release()



def emit_stage_moe(ctx, d_xin, d_xout, W, alpha=0.5):
    tk, A = ctx.tk, ctx.A
    NE = 8
    DFF = 4096
    NDP = DFF // 128
    TCH = 512
    NTC = T // TCH
    d_yt, d_wt, d_acc = W["d_yt"], W["d_wt"], W["d_acc"]

    A.mark()
    ctx.make_ident()
    xt_db = [A.alloc(1024, F32) for _ in range(2)]
    scol = A.alloc(32, F32)
    b2sb = A.alloc(1024, BF16, parts=8)
    wall = A.alloc(4096, BF16, parts=8)
    A.mark()
    ytmp_db = [A.alloc(1024, F32) for _ in range(2)]
    ybf_db = [A.alloc(1024, BF16) for _ in range(2)]
    ytev_db = [A.alloc(1024, BF16) for _ in range(2)]
    for i in range(NT):
        xt = xt_db[i % 2]
        yt = ytmp_db[i % 2]
        ybf = ybf_db[i % 2]
        row = i * 128
        tk.op("sp", lambda e, xt=xt, row=row: e.dma_start(
            xt.ap(), d_xin.t[row:row + 128, :]), reads=[d_xin], writes=[xt], dma=True, tag="xload")
        emit_ln(ctx, xt, yt, (i % 2) * 6)
        sc = scol.cview(i, 1)
        tk.op("vector", lambda e, yt=yt, sc=sc: e.tensor_add(sc.ap(), yt.cview(0, 1).ap(), yt.cview(1, 1).ap()),
              reads=[yt], writes=[sc], tag="scol")
        tk.op("vector", lambda e, yt=yt, ybf=ybf: e.tensor_copy(ybf.ap(), yt.ap()),
              reads=[yt], writes=[ybf], tag="ycast")
        pt = ctx.ps("B" if i % 2 == 0 else "W", 0, 1024, BF16)
        for j in range(NC):
            src = ybf.cview(j * 128, 128)
            dst = pt.cview(j * 128, 128)
            tk.op("tensor", lambda e, s=src, d=dst: e.transpose(d.ap(), s.ap(), ctx.ident.ap()),
                  reads=[src, ctx.ident], writes=[dst], tag="tr")
        ye = ytev_db[i % 2]
        tk.op("vector", lambda e, pt=pt, ye=ye: e.tensor_copy(ye.ap(), pt.ap()),
              reads=[pt], writes=[ye], tag="trev")
        tk.op("sp", lambda e, ye=ye, i=i: e.dma_start(
            d_yt.t[:, :].rearrange("p (j g) -> p j g", j=NC)[:, :, i * 128:(i + 1) * 128],
            ye.ap().rearrange("p (j c) -> p j c", j=NC)),
            reads=[ye], writes=[d_yt], dma=True, tag="ytout")

    def rint(dst, src, itmp):
        tk.op("vector", lambda e: e.tensor_copy(itmp.ap(), src.ap()), reads=[src], writes=[itmp], tag="cvt1")
        tk.op("vector", lambda e: e.tensor_copy(dst.ap(), itmp.ap()), reads=[itmp], writes=[dst], tag="cvt2")

    def floor_(dst, src, itmp, corr):
        rint(dst, src, itmp)
        tk.op("vector", lambda e: e.tensor_tensor(corr.ap(), dst.ap(), src.ap(), ALU.is_gt),
              reads=[dst, src], writes=[corr], tag="fgt")
        tk.op("vector", lambda e: e.tensor_sub(dst.ap(), dst.ap(), corr.ap()),
              reads=[dst, corr], writes=[dst], tag="fsub")

    sabs = A.alloc(32, F32)
    sgn = A.alloc(32, F32)
    fl = A.alloc(32, F32)
    itmp = A.alloc(32, I32)
    corr = A.alloc(32, F32)
    trc = A.alloc(32, F32)
    hb = A.alloc(32, F32)
    tk.op("scalar", lambda e: e.activation(sabs.ap(), scol.ap(), AF.Abs), reads=[scol], writes=[sabs], tag="habs")
    tk.op("scalar", lambda e: e.activation(sgn.ap(), scol.ap(), AF.Sign), reads=[scol], writes=[sgn], tag="hsgn")
    floor_(fl, sabs, itmp, corr)
    tk.op("vector", lambda e: e.tensor_mul(trc.ap(), fl.ap(), sgn.ap()), reads=[fl, sgn], writes=[trc], tag="htr")
    tk.op("vector", lambda e: e.tensor_single_scalar(hb.ap(), trc.ap(), 0.125, ALU.mult),
          reads=[trc], writes=[hb], tag="hdiv")
    floor_(fl, hb, itmp, corr)
    tk.op("vector", lambda e: e.scalar_tensor_tensor(hb.ap(), fl.ap(), -8.0, trc.ap(), ALU.mult, ALU.add),
          reads=[fl, trc], writes=[hb], tag="hmod")
    wmask = A.alloc(256, BF16)
    eq1 = A.alloc(32, F32)
    eq2 = A.alloc(32, F32)
    for e_ in range(NE):
        e2 = (e_ - 1) % NE
        tk.op("vector", lambda e, e_=e_: e.tensor_single_scalar(eq1.ap(), hb.ap(), float(e_), ALU.is_equal),
              reads=[hb], writes=[eq1], tag="meq1")
        tk.op("vector", lambda e, e2=e2: e.tensor_single_scalar(eq2.ap(), hb.ap(), float(e2), ALU.is_equal),
              reads=[hb], writes=[eq2], tag="meq2")
        dst = wmask.cview(e_ * 32, 32)
        tk.op("vector", lambda e, dst=dst: e.tensor_add(eq1.ap(), eq1.ap(), eq2.ap()),
              reads=[eq1, eq2], writes=[eq1], tag="madd")
        tk.op("vector", lambda e, dst=dst: e.tensor_single_scalar(dst.ap(), eq1.ap(), 0.5, ALU.mult),
              reads=[eq1], writes=[dst], tag="mhalf")
    wtev = A.alloc(128, BF16)
    for b in range(2):
        pt = ctx.ps("B", 0, 128, BF16)
        src = wmask.cview(b * 128, 128)
        tk.op("tensor", lambda e, src=src, pt=pt: e.transpose(pt.ap(), src.ap(), ctx.ident.ap()),
              reads=[src, ctx.ident], writes=[pt], tag="wtr")
        tk.op("vector", lambda e, pt=pt: e.tensor_copy(wtev.ap(), pt.ap()), reads=[pt], writes=[wtev], tag="wtev")
        tk.op("sp", lambda e, b=b: e.dma_start(d_wt.t[b * 128:(b + 1) * 128, :], wtev.ap()),
              reads=[wtev], writes=[d_wt], dma=True, tag="wtout")

    tk.op("pool", lambda e: e.dma_start(b2sb.ap(), W["moe_b2"].t[:, :]),
          reads=[W["moe_b2"]], writes=[b2sb], dma=True, tag="b2load")
    tk.op("sp", lambda e: e.dma_start(wall.ap(), d_wt.t[:, :].rearrange("(e f) q -> e (f q)", f=32)),
          reads=[d_wt], writes=[wall], dma=True, tag="wallload")

    A.release()
    A.mark()
    w1sb = A.alloc(NC * DFF, BF16)
    w2sb = A.alloc(NDP * 1024, BF16)
    b1sb = A.alloc(32, F32)
    ff1buf = A.alloc(NDP * TCH, BF16)
    ysl_db = [A.alloc(NC * TCH, BF16)]
    wrep_db = [A.alloc(TCH, BF16) for _ in range(2)]
    rtmp_db = [A.alloc(TCH, BF16) for _ in range(2)]
    obuf_db = [A.alloc(TCH, F32) for _ in range(2)]

    for e_ in range(NE):
        tk.op("pool", lambda e, e_=e_: e.dma_start(
            w1sb.ap().rearrange("p (j d) -> p j d", j=NC),
            W["moe_w1"].t[e_, :, :].rearrange("(j p) d -> p j d", p=128)),
            reads=[W["moe_w1"]], writes=[w1sb], dma=True, tag="w1load")
        tk.op("pool", lambda e, e_=e_: e.dma_start(
            w2sb.ap().rearrange("p (j m) -> p j m", j=NDP),
            W["moe_w2"].t[e_, :, :].rearrange("(j p) m -> p j m", p=128)),
            reads=[W["moe_w2"]], writes=[w2sb], dma=True, tag="w2load")
        tk.op("sp", lambda e, e_=e_: e.dma_start(b1sb.ap(), W["moe_b1pj"].t[e_, :, :]),
              reads=[W["moe_b1pj"]], writes=[b1sb], dma=True, tag="b1load")
        for tc in range(NTC):
            t0 = tc * TCH
            ysl = ysl_db[0]
            wrep = wrep_db[tc % 2]
            tk.op("sp", lambda e, ysl=ysl, t0=t0: e.dma_start(
                ysl.ap().rearrange("p (j c) -> p j c", j=NC),
                d_yt.t[:, :].rearrange("p (j g) -> p j g", j=NC)[:, :, t0:t0 + TCH]),
                reads=[d_yt], writes=[ysl], dma=True, tag="yslice")
            tk.op("sp", lambda e, wrep=wrep, e_=e_, t0=t0: e.dma_start(
                wrep.ap(),
                d_wt.t[:, :].rearrange("r q -> (r q)")[e_ * T + t0:e_ * T + t0 + TCH]
                .unsqueeze(0).broadcast_to((128, TCH))),
                reads=[d_wt], writes=[wrep], dma=True, tag="wrep")
            for dp in range(NDP):
                ps = ctx.ps("A" if dp % 2 == 0 else "B", 0, 512)
                pairs = [(w1sb.cview(jc * DFF + dp * 128, 128), ysl.cview(jc * TCH, TCH))
                         for jc in range(NC)]
                emit_mm_acc(ctx, ps, pairs, tag="ff1")
                rt = rtmp_db[dp % 2]
                b1c = b1sb.cview(dp, 1)
                tk.op("scalar", lambda e, ps=ps, rt=rt, b1c=b1c: e.activation(
                    rt.ap(), ps.ap(), AF.Relu, bias=b1c.ap()),
                    reads=[ps, b1sb], writes=[rt], tag="ff1relu")
                dst = ff1buf.cview(dp * TCH, TCH)
                tk.op("vector", lambda e, rt=rt, dst=dst, wrep=wrep: e.tensor_mul(
                    dst.ap(), rt.ap(), wrep.ap()),
                    reads=[rt, wrep], writes=[dst], tag="ff1mask")
            for mp in range(NC):
                ps = ctx.ps("A" if mp % 2 == 0 else "B", 0, 512)
                pairs = [(w2sb.cview(jc * 1024 + mp * 128, 128), ff1buf.cview(jc * TCH, TCH))
                         for jc in range(NDP)]
                if e_ == NE - 1:
                    pairs.append((b2sb.cview(mp * 128, 128), wall.cview(t0, TCH)))
                emit_mm_acc(ctx, ps, pairs, tag="ff2")
                ob = obuf_db[mp % 2]
                tk.op("vector", lambda e, ps=ps, ob=ob: e.tensor_copy(ob.ap(), ps.ap()),
                      reads=[ps], writes=[ob], tag="ff2ev")
                accsl = d_acc.t[:, mp * T + t0:mp * T + t0 + TCH]
                if e_ == 0:
                    tk.op("pool", lambda e, ob=ob, accsl=accsl: e.dma_start(accsl, ob.ap()),
                          reads=[ob], writes=[d_acc], dma=True, tag="acc0")
                else:
                    tk.op("pool", lambda e, ob=ob, accsl=accsl: e.dma_start(
                        accsl, ob.ap(), accum_op=ALU.add),
                        reads=[ob, d_acc], writes=[d_acc], dma=True, tag="accadd")

    A.release()
    accbf = A.alloc(NC * T, BF16)
    for mp in range(NC):
        tk.op("pool", lambda e, mp=mp: e.dma_start(
            accbf.cview(mp * T, T).ap(), d_acc.t[:, mp * T:(mp + 1) * T]),
            reads=[d_acc], writes=[accbf], dma=True, tag="accload")
    xo_db = [A.alloc(1024, F32) for _ in range(2)]
    for i in range(NT):
        pt = ctx.ps("B" if i % 2 == 0 else "W", 0, 1024, BF16)
        for j in range(NC):
            src = accbf.cview(j * T + i * 128, 128)
            dst = pt.cview(j * 128, 128)
            tk.op("tensor", lambda e, s=src, d=dst: e.transpose(d.ap(), s.ap(), ctx.ident.ap()),
                  reads=[src, ctx.ident], writes=[dst], tag="btr")
        xt = xt_db[i % 2]
        xo = xo_db[i % 2]
        row = i * 128
        tk.op("sp", lambda e, xt=xt, row=row: e.dma_start(
            xt.ap(), d_xin.t[row:row + 128, :]), reads=[d_xin], writes=[xt], dma=True, tag="xload2")
        tk.op("vector", lambda e, pt=pt, xt=xt, xo=xo: e.scalar_tensor_tensor(
            xo.ap(), pt.ap(), float(alpha), xt.ap(), ALU.mult, ALU.add),
            reads=[pt, xt], writes=[xo], tag="resadd")
        tk.op("sp", lambda e, xo=xo, row=row: e.dma_start(
            d_xout.t[row:row + 128, :], xo.ap()), reads=[xo], writes=[d_xout], dma=True, tag="xstore")
    A.release()



ATTN_L = 99


def emit_stage_attn(ctx, d_xin, d_xout, W):
    tk, A = ctx.tk, ctx.A
    A.mark()
    ctx.make_ident()
    d_bh = W["b_h"]
    d_bh2 = W["b_h2"]
    d_bqt = W["b_qt"]
    d_bpt = W["b_pt"]
    d_rel = W["b_relt"]

    xt_db = [A.alloc(1024, F32) for _ in range(2)]
    yt_db = [A.alloc(1024, F32) for _ in range(2)]

    for i in range(NT):
        xt = xt_db[i % 2]
        yt = yt_db[i % 2]
        row = i * 128
        tk.op("sp", lambda e, xt=xt, row=row: e.dma_start(
            xt.ap(), d_xin.t[row:row + 128, :]), reads=[d_xin], writes=[xt], dma=True, tag="xload")
        emit_ln(ctx, xt, yt, (i % 2) * 6)
        tk.op("sp", lambda e, yt=yt, row=row: e.dma_start(
            d_bh.t[row:row + 128, :], yt.ap()), reads=[yt], writes=[d_bh], dma=True, tag="h0out")

    for l in range(min(6, ATTN_L)):
        A.mark()
        ktT = A.alloc(NC * T, BF16)
        A.mark()
        hT = A.alloc(NC * T, BF16)
        hbf_db = [A.alloc(1024, BF16) for _ in range(2)]
        for i in range(NT):
            hbf = hbf_db[i % 2]
            tk.op("pool", lambda e, hbf=hbf, i=i: e.dma_start(
                hbf.ap(), d_bh.t[i * 128:(i + 1) * 128, :]),
                reads=[d_bh], writes=[hbf], dma=True, tag="hbfload")
            emit_transpose_tile(ctx, hT, T, hbf, i)
        wq = A.alloc(NC * 1024, BF16)
        wk = A.alloc(NC * 1024, BF16)
        for nm, wt in (("b_wqT", wq), ("b_wkT", wk)):
            tk.op("pool", lambda e, nm=nm, wt=wt, l=l: e.dma_start(
                wt.ap().rearrange("p (j d) -> p j d", j=NC),
                W[nm].t[l, :, :].rearrange("(j p) d -> p j d", p=128)),
                reads=[W[nm]], writes=[wt], dma=True, tag="wqk")
        bqc = A.alloc(8, F32)
        bkc = A.alloc(8, F32)
        tk.op("sp", lambda e, l=l: e.dma_start(bqc.ap(), W["b_bqpj"].t[l, :, :]),
              reads=[W["b_bqpj"]], writes=[bqc], dma=True)
        tk.op("sp", lambda e, l=l: e.dma_start(bkc.ap(), W["b_bkpj"].t[l, :, :]),
              reads=[W["b_bkpj"]], writes=[bkc], dma=True)
        qtev_db = [A.alloc(512, BF16) for _ in range(2)]
        for dp in range(NC):
            for f in range(8):
                par = (dp * 8 + f) % 2
                ps = ctx.ps("A" if par == 0 else "B", 0, 512)
                pairs = [(wq.cview(jc * 1024 + dp * 128, 128), hT.cview(jc * T + f * 512, 512))
                         for jc in range(NC)]
                emit_mm_acc(ctx, ps, pairs, tag="qproj")
                qte = qtev_db[par]
                tk.op("vector", lambda e, ps=ps, qte=qte, dp=dp: e.tensor_scalar(
                    qte.ap(), ps.ap(), bqc.cview(dp, 1).ap(), 0.125, ALU.add, ALU.mult),
                    reads=[ps, bqc], writes=[qte], tag="qev")
                tk.op("sp", lambda e, qte=qte, dp=dp, f=f: e.dma_start(
                    d_bqt.t[dp * 128:(dp + 1) * 128, f * 512:(f + 1) * 512], qte.ap()),
                    reads=[qte], writes=[d_bqt], dma=True, tag="qtout")
        for dp in range(NC):
            for f in range(8):
                par = (dp * 8 + f) % 2
                ps = ctx.ps("A" if par == 0 else "B", 0, 512)
                pairs = [(wk.cview(jc * 1024 + dp * 128, 128), hT.cview(jc * T + f * 512, 512))
                         for jc in range(NC)]
                emit_mm_acc(ctx, ps, pairs, tag="kproj")
                dst = ktT.cview(dp * T + f * 512, 512)
                tk.op("vector", lambda e, ps=ps, dst=dst, dp=dp: e.tensor_scalar(
                    dst.ap(), ps.ap(), bkc.cview(dp, 1).ap(), None, ALU.add),
                    reads=[ps, bkc], writes=[dst], tag="kev")
        A.release()

        A.mark()
        E8 = A.alloc(8 * T, BF16)
        w2sb = Tile(A.t8, A.pool, E8.lo, E8.lo + T * 4, 0, 128, E8.lo // 4, T, F32)
        wavg = A.alloc(T, F32)
        relt = A.alloc(T, F32)
        qts_db = [A.alloc(NC * 128, BF16) for _ in range(2)]
        zparts = A.alloc(64, F32)
        zsum = A.alloc(16, F32)
        dh_all = A.alloc(8 * 128, BF16)
        pb = A.alloc(T, BF16)
        pb2 = pb
        ptev = A.alloc(T, BF16)
        z2c = A.alloc(4, F32)
        for qb in range(NT):
            qts = qts_db[qb % 2]
            tk.op("pool", lambda e, qts=qts, qb=qb: e.dma_start(
                qts.ap().rearrange("p (j c) -> p j c", j=NC),
                d_bqt.t[:, :].rearrange("(j p) t -> p j t", p=128)[:, :, qb * 128:(qb + 1) * 128]),
                reads=[d_bqt], writes=[qts], dma=True, tag="qtsload")
            tk.op("sp", lambda e, qb=qb: e.dma_start(
                relt.ap(), d_rel.t[:, 3968 - qb * 128:3968 - qb * 128 + T]),
                reads=[d_rel], writes=[relt], dma=True, tag="relload")
            for grp in range(2):
                for h8 in range(8):
                    h = grp * 8 + h8
                    lhs = qts.cview((h // 2) * 128, 128).pview((h % 2) * 64, 64)
                    for c4 in range(4):
                        ps = ctx.ps("A" if c4 % 2 == 0 else "B", 0, 1024)
                        for ff in range(2):
                            rhs = ktT.cview((h // 2) * T + c4 * 1024 + ff * 512, 512).pview((h % 2) * 64, 64)
                            pss = ps.cview(ff * 512, 512)
                            tk.op("tensor", lambda e, pss=pss, lhs=lhs, rhs=rhs: e.matmul(
                                pss.ap(), lhs.ap(), rhs.ap(), start=True, stop=True),
                                reads=[qts, ktT], writes=[pss], tag="score")
                        dst = E8.cview(h8 * T + c4 * 1024, 1024)
                        zp = zparts.cview(h8 * 4 + c4, 1)
                        tk.op("scalar", lambda e, ps=ps, dst=dst, zp=zp: e.activation(
                            dst.ap(), ps.ap(), AF.Exp, accum_out=zp.ap()),
                            reads=[ps], writes=[dst, zp], tag="exp1")
                tk.op("vector", lambda e: e.reduce_sum(
                    zsum.cview(0, 8).ap().rearrange("p (a o) -> p a o", o=1),
                    zparts.cview(0, 32).ap().rearrange("p (a b) -> p a b", a=8), AX.X),
                    reads=[zparts], writes=[zsum], tag="zred")
                tk.op("vector", lambda e: e.tensor_single_scalar(
                    zsum.cview(0, 8).ap(), zsum.cview(0, 8).ap(), 1e-37, ALU.add),
                    reads=[zsum], writes=[zsum], tag="zeps")
                tk.op("vector", lambda e: e.reciprocal(zsum.cview(8, 8).ap(), zsum.cview(0, 8).ap()),
                      reads=[zsum], writes=[zsum], tag="zrec")
                for h8 in range(8):
                    dh = dh_all.cview(h8 * 128, 128)
                    tk.op("vector", lambda e, dh=dh, h8=h8: e.tensor_scalar(
                        dh.ap(), ctx.eqm.ap(), zsum.cview(8 + h8, 1).ap(), 0.0625,
                        ALU.mult, ALU.mult),
                        reads=[ctx.eqm, zsum], writes=[dh], tag="dh")
                for half in range(2):
                    for h8 in range(8):
                        for fq in range(4):
                            o = Tile(ctx.ps_wide, "psum", 8192, 16384, 0, 128, fq * 512, 512, F32)
                            tk.op("tensor", lambda e, o=o, h8=h8, half=half, fq=fq: e.matmul(
                                o.ap(), dh_all.cview(h8 * 128, 128).ap(),
                                E8.cview(h8 * T + half * 2048 + fq * 512, 512).ap(),
                                start=(h8 == 0), stop=(h8 == 7)),
                                reads=[dh_all, E8], writes=[o], tag="diag")
                    wv = Tile(ctx.ps_wide, "psum", 8192, 16384, 0, 128, 0, 2048, F32)
                    wdst = wavg.cview(half * 2048, 2048)
                    if grp == 0:
                        tk.op("vector", lambda e, wv=wv, wdst=wdst: e.tensor_copy(wdst.ap(), wv.ap()),
                              reads=[wv], writes=[wdst], tag="wavg0")
                    else:
                        tk.op("vector", lambda e, wv=wv, wdst=wdst: e.tensor_add(
                            wdst.ap(), wdst.ap(), wv.ap()),
                            reads=[wv, wdst], writes=[wdst], tag="wavg1")
            tk.op("vector", lambda e: e.tensor_add(w2sb.ap(), wavg.ap(), relt.ap()),
                  reads=[wavg, relt], writes=[w2sb], tag="rel_add")
            tk.op("scalar", lambda e: e.activation(pb.ap(), w2sb.ap(), AF.Exp,
                                                   accum_out=z2c.cview(0, 1).ap()),
                  reads=[w2sb], writes=[pb, z2c], tag="exp2")
            tk.op("vector", lambda e: e.tensor_single_scalar(
                z2c.cview(0, 1).ap(), z2c.cview(0, 1).ap(), 1e-37, ALU.add),
                reads=[z2c], writes=[z2c], tag="z2eps")
            tk.op("vector", lambda e: e.reciprocal(z2c.cview(1, 1).ap(), z2c.cview(0, 1).ap()),
                  reads=[z2c], writes=[z2c], tag="z2rec")
            tk.op("vector", lambda e: e.tensor_scalar(pb2.ap(), pb.ap(), z2c.cview(1, 1).ap(), None, ALU.mult),
                  reads=[pb, z2c], writes=[pb2], tag="pnorm")
            for mc in range(32):
                pt = ctx.ps("B" if mc % 2 == 0 else "W", 0, 1024, BF16)
                ptc = pt.cview(0, 128)
                src = pb2.cview(mc * 128, 128)
                tk.op("tensor", lambda e, src=src, ptc=ptc: e.transpose(ptc.ap(), src.ap(), ctx.ident.ap()),
                      reads=[src, ctx.ident], writes=[ptc], tag="ptr")
                dst = ptev.cview(mc * 128, 128)
                tk.op("vector", lambda e, ptc=ptc, dst=dst: e.tensor_copy(dst.ap(), ptc.ap()),
                      reads=[ptc], writes=[dst], tag="ptev")
            tk.op("sp", lambda e, qb=qb: e.dma_start(d_bpt.t[qb, :, :], ptev.ap()),
                  reads=[ptev], writes=[d_bpt], dma=True, tag="ptout")
        A.release()
        A.release()

        A.mark()
        hrb = A.alloc(32 * 1024, BF16)
        for mc in range(NT):
            tk.op("pool", lambda e, mc=mc: e.dma_start(
                hrb.cview(mc * 1024, 1024).ap(), d_bh.t[mc * 128:(mc + 1) * 128, :]),
                reads=[d_bh], writes=[hrb], dma=True, tag="hrbload")
        lw1 = A.alloc(1024, F32)
        lb1 = A.alloc(1024, F32)
        lw2 = A.alloc(1024, F32)
        lb2 = A.alloc(1024, F32)
        for nm, tile in (("b_ln1w", lw1), ("b_ln1b", lb1), ("b_ln2w", lw2), ("b_ln2b", lb2)):
            tk.op("sp", lambda e, nm=nm, tile=tile, l=l: e.dma_start(
                tile.ap(), W[nm].t[l:l + 1, :].broadcast_to((128, 1024))),
                reads=[W[nm]], writes=[tile], dma=True, tag="lnw")
        pts_db = [A.alloc(32 * 128, BF16) for _ in range(2)]
        z1_db = [A.alloc(1024, F32) for _ in range(2)]
        h2_db = [A.alloc(1024, F32) for _ in range(2)]
        for qb in range(NT):
            pts = pts_db[qb % 2]
            tk.op("pool", lambda e, pts=pts, qb=qb: e.dma_start(pts.ap(), d_bpt.t[qb, :, :]),
                  reads=[d_bpt], writes=[pts], dma=True, tag="ptsload")
            xt = xt_db[qb % 2]
            tk.op("sp", lambda e, xt=xt, qb=qb: e.dma_start(
                xt.ap(), d_bh.t[qb * 128:(qb + 1) * 128, :]),
                reads=[d_bh], writes=[xt], dma=True, tag="hload3")
            z1 = z1_db[qb % 2]
            for eh in range(2):
                ps = ctx.ps("A" if eh == 0 else "B", 0, 512)
                pairs = [(pts.cview(mc * 128, 128), hrb.cview(mc * 1024 + eh * 512, 512))
                         for mc in range(NT)]
                emit_mm_acc(ctx, ps, pairs, tag="sa")
                tk.op("vector", lambda e, ps=ps, z1=z1, xt=xt, eh=eh: e.tensor_add(
                    z1.cview(eh * 512, 512).ap(), ps.ap(), xt.cview(eh * 512, 512).ap()),
                    reads=[ps, xt], writes=[z1], tag="saadd")
            h2t = h2_db[qb % 2]
            emit_ln(ctx, z1, z1, (qb % 2) * 6, affine=(lw1, lb1))
            emit_ln(ctx, z1, h2t, (qb % 2) * 6 + 12 - 12, affine=(lw2, lb2))
            tk.op("sp", lambda e, h2t=h2t, qb=qb: e.dma_start(
                d_bh2.t[qb * 128:(qb + 1) * 128, :], h2t.ap()),
                reads=[h2t], writes=[d_bh2], dma=True, tag="h2out")
        A.release()

        A.mark()
        w1T = A.alloc(NC * 4096, BF16)
        w2sb4 = A.alloc(32 * 1024, BF16)
        tk.op("pool", lambda e, l=l: e.dma_start(
            w1T.ap().rearrange("p (j d) -> p j d", j=NC),
            W["b_w1T"].t[l, :, :].rearrange("(j p) d -> p j d", p=128)),
            reads=[W["b_w1T"]], writes=[w1T], dma=True, tag="w1load")
        tk.op("pool", lambda e, l=l: e.dma_start(
            w2sb4.ap().rearrange("p (j m) -> p j m", j=32),
            W["b_w2T"].t[l, :, :].rearrange("(j p) m -> p j m", p=128)),
            reads=[W["b_w2T"]], writes=[w2sb4], dma=True, tag="w2load")
        b1c = A.alloc(32, F32)
        tk.op("sp", lambda e, l=l: e.dma_start(b1c.ap(), W["b_b1pj"].t[l, :, :]),
              reads=[W["b_b1pj"]], writes=[b1c], dma=True)
        b2bc = A.alloc(1024, BF16)
        tk.op("pool", lambda e, l=l: e.dma_start(
            b2bc.ap(), W["b_b2"].t[l:l + 1, :].broadcast_to((128, 1024))),
            reads=[W["b_b2"]], writes=[b2bc], dma=True, tag="b2bc")
        lw3 = A.alloc(1024, BF16)
        lb3 = A.alloc(1024, BF16)
        for nm, tile in (("b_ln3w", lw3), ("b_ln3b", lb3)):
            tk.op("pool", lambda e, nm=nm, tile=tile, l=l: e.dma_start(
                tile.ap(), W[nm].t[l:l + 1, :].broadcast_to((128, 1024))),
                reads=[W[nm]], writes=[tile], dma=True, tag="lnw3")
        h2ts = A.alloc(NC * 512, BF16)
        h2bf_db = [A.alloc(1024, BF16)]
        ff1buf = A.alloc(32 * 512, BF16)
        fft_db = [A.alloc(1024, F32)]
        h24_db = [A.alloc(1024, F32)]
        for f in range(8):
            for i4 in range(4):
                h2bf = h2bf_db[0]
                row = f * 512 + i4 * 128
                tk.op("pool", lambda e, h2bf=h2bf, row=row: e.dma_start(
                    h2bf.ap(), d_bh2.t[row:row + 128, :]),
                    reads=[d_bh2], writes=[h2bf], dma=True, tag="h2bfload")
                emit_transpose_tile(ctx, h2ts, 512, h2bf, i4)
            for dp in range(32):
                ps = ctx.ps("A" if dp % 2 == 0 else "B", 0, 512)
                pairs = [(w1T.cview(jc * 4096 + dp * 128, 128), h2ts.cview(jc * 512, 512))
                         for jc in range(NC)]
                emit_mm_acc(ctx, ps, pairs, tag="ff1")
                dst = ff1buf.cview(dp * 512, 512)
                tk.op("scalar", lambda e, ps=ps, dst=dst, dp=dp: e.activation(
                    dst.ap(), ps.ap(), AF.Relu, bias=b1c.cview(dp, 1).ap()),
                    reads=[ps, b1c], writes=[dst], tag="ff1relu")
            for tc4 in range(4):
                fft = fft_db[0]
                h24 = h24_db[0]
                hn = fft
                row = f * 512 + tc4 * 128
                tk.op("sp", lambda e, h24=h24, row=row: e.dma_start(
                    h24.ap(), d_bh2.t[row:row + 128, :]),
                    reads=[d_bh2], writes=[h24], dma=True, tag="h24load")
                for eh in range(2):
                    ps = ctx.ps("A" if eh == 0 else "B", 0, 512)
                    pairs = [(ff1buf.cview(dp * 512 + tc4 * 128, 128),
                              w2sb4.cview(dp * 1024 + eh * 512, 512)) for dp in range(32)]
                    emit_mm_acc(ctx, ps, pairs, tag="ff2")
                    tk.op("vector", lambda e, ps=ps, fft=fft, eh=eh: e.scalar_tensor_tensor(
                        fft.cview(eh * 512, 512).ap(), ps.ap(), 1.0,
                        b2bc.cview(eh * 512, 512).ap(), ALU.mult, ALU.add),
                        reads=[ps, b2bc], writes=[fft], tag="ff2ev")
                tk.op("vector", lambda e, fft=fft, h24=h24: e.tensor_add(
                    fft.ap(), fft.ap(), h24.ap()),
                    reads=[fft, h24], writes=[fft], tag="ffres")
                emit_ln(ctx, fft, hn, (tc4 % 2) * 6, affine=(lw3, lb3))
                tk.op("sp", lambda e, hn=hn, row=row: e.dma_start(
                    d_bh.t[row:row + 128, :], hn.ap()),
                    reads=[hn], writes=[d_bh], dma=True, tag="hnout")
        A.release()

    ho_db = [A.alloc(1024, F32) for _ in range(2)]
    xo_db = [A.alloc(1024, F32) for _ in range(2)]
    for i in range(NT):
        xt = xt_db[i % 2]
        ho = ho_db[i % 2]
        xo = xo_db[i % 2]
        row = i * 128
        tk.op("sp", lambda e, xt=xt, row=row: e.dma_start(
            xt.ap(), d_xin.t[row:row + 128, :]), reads=[d_xin], writes=[xt], dma=True, tag="xload9")
        tk.op("sp", lambda e, ho=ho, row=row: e.dma_start(
            ho.ap(), d_bh.t[row:row + 128, :]), reads=[d_bh], writes=[ho], dma=True, tag="hload9")
        tk.op("vector", lambda e, xt=xt, ho=ho, xo=xo: e.tensor_add(
            xo.ap(), xt.ap(), ho.ap()), reads=[xt, ho], writes=[xo], tag="fres")
        tk.op("sp", lambda e, xo=xo, row=row: e.dma_start(
            d_xout.t[row:row + 128, :], xo.ap()), reads=[xo], writes=[d_xout], dma=True, tag="xstore9")
    A.release()



DI = 2048
NJI = DI // 128


MAMBA_PH = 99
MAMBA_V2 = 0


def emit_stage_mamba(ctx, d_xin, d_xout, W):
    def mp(blk, p):
        return blk * 7 + p < MAMBA_PH
    tk, A = ctx.tk, ctx.A
    A.mark()
    ctx.make_ident()

    d_yt = W["mam_yt"]
    d_u = W["mam_u"]
    d_z = W["mam_z"]
    d_dl = W["mam_dl"]
    d_ys = W["mam_ys"]
    d_yo = W["mam_yo"]

    rrep = A.alloc(16 * 128, BF16)
    s8 = A.alloc(8 * 64, BF16)
    r16b = A.alloc(128, BF16, parts=32)
    r16c = A.alloc(128, BF16, parts=32)
    tk.op("pool", lambda e: e.dma_start(rrep.ap().rearrange("p (g c) -> p g c", g=16),
                                      W["mam_rrep"].t[:, :, :].rearrange("g c p -> c g p")),
          reads=[W["mam_rrep"]], writes=[rrep], dma=True, tag="rrep")
    tk.op("pool", lambda e: e.dma_start(s8.ap().rearrange("p (q c) -> p q c", q=8),
                                      W["mam_s8"].t[:, :, :].rearrange("q c p -> c q p")),
          reads=[W["mam_s8"]], writes=[s8], dma=True, tag="s8")
    tk.op("pool", lambda e: e.dma_start(r16b.ap(), W["mam_r16b"].t[:, :]),
          reads=[W["mam_r16b"]], writes=[r16b], dma=True)
    tk.op("pool", lambda e: e.dma_start(r16c.ap(), W["mam_r16c"].t[:, :]),
          reads=[W["mam_r16c"]], writes=[r16c], dma=True)

    xt_db = [A.alloc(1024, F32) for _ in range(2)]
    ybf_db = [A.alloc(1024, BF16) for _ in range(2)]
    ytev_db = [A.alloc(1024, BF16) for _ in range(2)]
    ytmp_db = [A.alloc(1024, F32) for _ in range(2)]

    for i in range(NT):
        xt = xt_db[i % 2]
        yt = ytmp_db[i % 2]
        ybf = ybf_db[i % 2]
        row = i * 128
        tk.op("sp", lambda e, xt=xt, row=row: e.dma_start(
            xt.ap(), d_xin.t[row:row + 128, :]), reads=[d_xin], writes=[xt], dma=True, tag="xload")
        emit_ln(ctx, xt, yt, (i % 2) * 6)
        tk.op("vector", lambda e, yt=yt, ybf=ybf: e.tensor_copy(ybf.ap(), yt.ap()),
              reads=[yt], writes=[ybf], tag="ycast")
        pt = ctx.ps("B" if i % 2 == 0 else "W", 0, 1024, BF16)
        for j in range(NC):
            src = ybf.cview(j * 128, 128)
            dst = pt.cview(j * 128, 128)
            tk.op("tensor", lambda e, s=src, d=dst: e.transpose(d.ap(), s.ap(), ctx.ident.ap()),
                  reads=[src, ctx.ident], writes=[dst], tag="tr")
        ye = ytev_db[i % 2]
        tk.op("vector", lambda e, pt=pt, ye=ye: e.tensor_copy(ye.ap(), pt.ap()),
              reads=[pt], writes=[ye], tag="trev")
        tk.op("sp", lambda e, ye=ye, i=i: e.dma_start(
            d_yt.t[:, :].rearrange("(j p) t -> p j t", p=128)[:, :, i * 128:(i + 1) * 128],
            ye.ap().rearrange("p (j c) -> p j c", j=NC)),
            reads=[ye], writes=[d_yt], dma=True, tag="ytout")

    cur = A.alloc(NC * T, BF16)
    tk.op("sp", lambda e: e.dma_start(
        cur.ap().rearrange("p (j t) -> p j t", j=NC),
        d_yt.t[:, :].rearrange("(j p) t -> p j t", p=128)),
        reads=[d_yt], writes=[cur], dma=True, tag="cur0")

    for blk in range(2):
        A.mark()
        apj = A.alloc(256, F32)
        dpj = A.alloc(16, F32)
        cwpj = A.alloc(64, F32)
        cbpj = A.alloc(16, F32)
        dtbpj = A.alloc(16, F32)
        for name, tile in (("mam_apj", apj), ("mam_dpj", dpj), ("mam_cwpj", cwpj),
                           ("mam_cbpj", cbpj), ("mam_dtbpj", dtbpj)):
            tk.op("sp", lambda e, name=name, tile=tile, blk=blk: e.dma_start(
                tile.ap(), W[name].t[blk, :, :]), reads=[W[name]], writes=[tile], dma=True, tag=name)
        acol = A.alloc(256, F32)
        tk.op("scalar", lambda e: e.activation(acol.ap(), apj.ap(), AF.Exp),
              reads=[apj], writes=[acol], tag="aexp")
        tk.op("vector", lambda e: e.tensor_single_scalar(acol.ap(), acol.ap(), -1.0, ALU.mult),
              reads=[acol], writes=[acol], tag="aneg")

        dmask = A.alloc(128, BF16)
        idx = A.alloc(128, F32)
        eqf = A.alloc(128, F32)
        tk.op("pool", lambda e: e.iota(idx.ap(), [[1, 128]], base=0, channel_multiplier=-1,
                                       allow_small_or_imprecise_dtypes=True), writes=[idx], tag="didx")
        tk.op("vector", lambda e: e.tensor_single_scalar(eqf.ap(), idx.ap(), 0.0, ALU.is_equal),
              reads=[idx], writes=[eqf], tag="deq")
        tk.op("vector", lambda e: e.tensor_copy(dmask.ap(), eqf.ap()), reads=[eqf], writes=[dmask], tag="deqc")

        if not mp(blk, 0):
            A.release()
            break
        A.mark()
        winT = A.alloc(NC * 4096, BF16)
        tk.op("pool", lambda e, blk=blk: e.dma_start(
            winT.ap().rearrange("p (j d) -> p j d", j=NC),
            W["mam_inT"].t[blk, :, :].rearrange("(j p) d -> p j d", p=128)),
            reads=[W["mam_inT"]], writes=[winT], dma=True, tag="winT")
        xcs = A.alloc(16 + T, BF16)
        usb_db = [A.alloc(T, BF16) for _ in range(2)]
        dk = [A.alloc(128, BF16) for _ in range(4)]
        tk.op("pool", lambda e: e.memset(xcs.cview(0, 16).ap(), 0.0), writes=[xcs.cview(0, 16)], tag="xcpad")
        for jo in range(NJI * 2):
            is_xc = jo < NJI
            dstb = usb_db[jo % 2] if not is_xc else None
            for f in range(8):
                ps = ctx.ps("A" if f % 2 == 0 else "B", 0, 512)
                pairs = [(winT.cview(jc * 4096 + jo * 128, 128), cur.cview(jc * T + f * 512, 512))
                         for jc in range(NC)]
                emit_mm_acc(ctx, ps, pairs, tag="inproj")
                if is_xc:
                    dst = xcs.cview(16 + f * 512, 512)
                else:
                    dst = dstb.cview(f * 512, 512)
                tk.op("vector", lambda e, ps=ps, dst=dst: e.tensor_copy(dst.ap(), ps.ap()),
                      reads=[ps], writes=[dst], tag="xzev")
            if is_xc:
                j = jo
                for k in range(4):
                    wcol = cwpj.cview(j * 4 + k, 1)
                    tk.op("vector", lambda e, k=k, wcol=wcol: e.tensor_scalar(
                        dk[k].ap(), dmask.ap(), wcol.ap(), None, ALU.mult),
                        reads=[dmask, cwpj], writes=[dk[k]], tag="dkbuild")
                usb = usb_db[j % 2]
                for f in range(8):
                    ps = ctx.ps("A" if f % 2 == 0 else "B", 0, 512)
                    pairs = [(dk[k], xcs.cview(16 + f * 512 - 3 + k, 512)) for k in range(4)]
                    emit_mm_acc(ctx, ps, pairs, tag="conv")
                    bcol = cbpj.cview(j, 1)
                    tk.op("scalar", lambda e, ps=ps, usb=usb, f=f, bcol=bcol: e.activation(
                        usb.cview(f * 512, 512).ap(), ps.ap(), AF.Silu, bias=bcol.ap()),
                        reads=[ps, cbpj], writes=[usb], tag="usilu")
                tk.op("sp", lambda e, usb=usb, j=j: e.dma_start(
                    d_u.t[j * 128:(j + 1) * 128, :], usb.ap()),
                    reads=[usb], writes=[d_u], dma=True, tag="uout")
            else:
                j = jo - NJI
                tk.op("sp", lambda e, dstb=dstb, j=j: e.dma_start(
                    d_z.t[j * 128:(j + 1) * 128, :], dstb.ap()),
                    reads=[dstb], writes=[d_z], dma=True, tag="zout")
        A.release()

        if not mp(blk, 1):
            A.release()
            break
        xd1 = A.alloc(T, BF16, parts=64)
        xd2 = A.alloc(T, BF16, parts=32)
        A.mark()
        wx1 = A.alloc(16 * 64, BF16)
        wx2 = A.alloc(16 * 32, BF16)
        tk.op("pool", lambda e, blk=blk: e.dma_start(wx1.ap(), W["mam_x1T"].t[blk, :, :]),
              reads=[W["mam_x1T"]], writes=[wx1], dma=True, tag="wx1")
        tk.op("pool", lambda e, blk=blk: e.dma_start(wx2.ap(), W["mam_x2T"].t[blk, :, :]),
              reads=[W["mam_x2T"]], writes=[wx2], dma=True, tag="wx2")
        uin_db = [A.alloc(T, BF16) for _ in range(2)]
        for f in range(8 if MAMBA_V2 != 3 else 0):
            for which in range(2):
                if MAMBA_V2 == 1 and which == 1:
                    continue
                ps = ctx.ps("A" if which == 0 else "B", 0, 512)
                for jc in range(NJI):
                    ut = uin_db[jc % 2]
                    tk.op("sp", lambda e, ut=ut, jc=jc, f=f: e.dma_start(
                        ut.cview(f * 512, 512).ap(),
                        d_u.t[jc * 128:(jc + 1) * 128, f * 512:(f + 1) * 512]),
                        reads=[d_u], writes=[ut], dma=True, tag="uload")
                    if MAMBA_V2 == 1:
                        continue
                    if MAMBA_V2 == 2:
                        ut = uin_db[0]
                    if which == 0:
                        tk.op("tensor", lambda e, ps=ps, jc=jc, ut=ut, f=f: e.matmul(
                            ps.pview(0, 64).ap(), wx1.cview(jc * 64, 64).ap(),
                            ut.cview(f * 512, 512).ap(),
                            start=(jc == 0), stop=(jc == NJI - 1)),
                            reads=[wx1, ut], writes=[ps], tag="xproj1")
                    else:
                        tk.op("tensor", lambda e, ps=ps, jc=jc, ut=ut, f=f: e.matmul(
                            ps.pview(0, 32).ap(), wx2.cview(jc * 32, 32).ap(),
                            ut.cview(f * 512, 512).ap(),
                            start=(jc == 0), stop=(jc == NJI - 1)),
                            reads=[wx2, ut], writes=[ps], tag="xproj2")
                if MAMBA_V2 == 1:
                    continue
                if which == 0:
                    tk.op("vector", lambda e, ps=ps, f=f: e.tensor_copy(
                        xd1.cview(f * 512, 512).ap(), ps.pview(0, 64).ap()),
                        reads=[ps], writes=[xd1], tag="xd1ev")
                else:
                    tk.op("vector", lambda e, ps=ps, f=f: e.tensor_copy(
                        xd2.cview(f * 512, 512).ap(), ps.pview(0, 32).ap()),
                        reads=[ps], writes=[xd2], tag="xd2ev")
        A.release()

        if not mp(blk, 2):
            A.release()
            break
        A.mark()
        wdt = A.alloc(16 * 128, BF16)
        tk.op("pool", lambda e, blk=blk: e.dma_start(
            wdt.pview(0, 64).ap(), W["mam_dtT"].t[blk, :, :]),
            reads=[W["mam_dtT"]], writes=[wdt], dma=True, tag="wdt")
        t_abs = A.alloc(512, F32)
        t_exp = A.alloc(512, F32)
        t_ln = A.alloc(512, F32)
        t_mx = A.alloc(512, F32)
        dlout_db = [A.alloc(T, BF16) for _ in range(2)]
        for j in range(NJI):
            dlo = dlout_db[j % 2]
            for f in range(8):
                ps = ctx.ps("A" if f % 2 == 0 else "B", 0, 512)
                tk.op("tensor", lambda e, ps=ps, j=j, f=f: e.matmul(
                    ps.ap(), wdt.pview(0, 64).cview(j * 128, 128).ap(), xd1.cview(f * 512, 512).ap(),
                    start=True, stop=True),
                    reads=[wdt, xd1], writes=[ps], tag="dtproj")
                bcol = dtbpj.cview(j, 1)
                if MAMBA_V2 == 4:
                    tk.op("vector", lambda e, ps=ps, dlo=dlo, f=f: e.tensor_copy(
                        dlo.cview(f * 512, 512).ap(), ps.ap()),
                        reads=[ps], writes=[dlo], tag="dtev")
                    continue
                tk.op("scalar", lambda e, ps=ps, bcol=bcol: e.activation(
                    t_abs.ap(), ps.ap(), AF.Abs, bias=bcol.ap()),
                    reads=[ps, dtbpj], writes=[t_abs], tag="sp_abs")
                tk.op("scalar", lambda e: e.activation(t_exp.ap(), t_abs.ap(), AF.Exp, scale=-1.0),
                      reads=[t_abs], writes=[t_exp], tag="sp_exp")
                cone = ctx.constf(1.0)
                tk.op("scalar", lambda e: e.activation(t_ln.ap(), t_exp.ap(), AF.Ln, bias=1.0),
                      reads=[t_exp, cone], writes=[t_ln], tag="sp_ln")
                if MAMBA_V2 == 5:
                    tk.op("vector", lambda e, dlo=dlo, f=f: e.tensor_copy(
                        dlo.cview(f * 512, 512).ap(), t_ln.ap()),
                        reads=[t_ln], writes=[dlo], tag="dtev5")
                    continue
                tk.op("vector", lambda e, ps=ps, bcol=bcol: e.tensor_scalar(
                    t_mx.ap(), ps.ap(), bcol.ap(), 0.0, ALU.add, ALU.max),
                    reads=[ps, dtbpj], writes=[t_mx], tag="sp_max")
                tk.op("vector", lambda e, dlo=dlo, f=f: e.tensor_add(
                    dlo.cview(f * 512, 512).ap(), t_mx.ap(), t_ln.ap()),
                    reads=[t_mx, t_ln], writes=[dlo], tag="sp_add")
            tk.op("sp", lambda e, dlo=dlo, j=j: e.dma_start(
                d_dl.t[j * 128:(j + 1) * 128, :], dlo.ap()),
                reads=[dlo], writes=[d_dl], dma=True, tag="dlout")
        A.release()

        if not mp(blk, 3):
            A.release()
            break
        A.mark()
        brep = A.alloc(T, BF16)
        crep = A.alloc(T, BF16)
        for f in range(8):
            psb = ctx.ps("A", 0, 512)
            tk.op("tensor", lambda e, psb=psb, f=f: e.matmul(
                psb.ap(), r16b.ap(), xd2.cview(f * 512, 512).ap(), start=True, stop=True),
                reads=[r16b, xd2], writes=[psb], tag="brep")
            tk.op("scalar", lambda e, psb=psb, f=f: e.activation(
                brep.cview(f * 512, 512).ap(), psb.ap(), AF.Copy),
                reads=[psb], writes=[brep], tag="brepev")
            psc = ctx.ps("B", 0, 512)
            tk.op("tensor", lambda e, psc=psc, f=f: e.matmul(
                psc.ap(), r16c.ap(), xd2.cview(f * 512, 512).ap(), start=True, stop=True),
                reads=[r16c, xd2], writes=[psc], tag="crep")
            tk.op("scalar", lambda e, psc=psc, f=f: e.activation(
                crep.cview(f * 512, 512).ap(), psc.ap(), AF.Copy),
                reads=[psc], writes=[crep], tag="crepev")

        SEC = 2048
        if not mp(blk, 4):
            A.release()
            A.release()
            break
        dl_db = [A.alloc(SEC, BF16) for _ in range(2)]
        u_db = [A.alloc(SEC, BF16) for _ in range(2)]
        e_db = [A.alloc(SEC, BF16) for _ in range(2)]
        da_db = [A.alloc(SEC, BF16) for _ in range(2)]
        dbu_db = [A.alloc(SEC, BF16) for _ in range(2)]
        h_db = [A.alloc(SEC, BF16) for _ in range(2)]
        ch_db = [A.alloc(SEC, BF16) for _ in range(2)]
        carry = A.alloc(256, F32)
        ysev = A.alloc(2048, BF16, parts=64)
        NSC = SEC // 512
        pend_ssum = []

        def flush_ssum(item):
            g, chg, ftsl0 = item
            q = g % 8
            for fq in range(NSC):
                o = Tile(ctx.ps_wide, "psum", 8192, 16384, 0, 64, fq * 512, 512, F32)
                tk.op("tensor", lambda e, o=o, q=q, fq=fq, chg=chg: e.matmul(
                    o.ap(), s8.cview(q * 64, 64).ap(), chg.cview(fq * 512, 512).ap(),
                    start=(q == 0), stop=(q == 7)),
                    reads=[s8, chg], writes=[o], tag="ssum")
            if q == 7:
                d0 = 8 * (g - 7)
                src = Tile(ctx.ps_wide, "psum", 8192, 16384, 0, 64, 0, 2048, F32)
                tk.op("vector", lambda e, src=src: e.tensor_copy(
                    ysev.pview(0, 64).ap(), src.ap()),
                    reads=[src], writes=[ysev], tag="ysev")
                tk.op("sp", lambda e, d0=d0, ftsl0=ftsl0: e.dma_start(
                    d_ys.t[d0:d0 + 64, ftsl0:ftsl0 + SEC], ysev.pview(0, 64).ap()),
                    reads=[ysev], writes=[d_ys], dma=True, tag="ysout")

        for sec in range(2):
            tsl0 = sec * SEC
            for j in range(NJI):
                dl_j = dl_db[j % 2]
                u_j = u_db[j % 2]
                e_j = e_db[j % 2]
                tk.op("sp", lambda e, j=j, tsl0=tsl0, dl_j=dl_j: e.dma_start(
                    dl_j.ap(), d_dl.t[j * 128:(j + 1) * 128, tsl0:tsl0 + SEC]),
                    reads=[d_dl], writes=[dl_j], dma=True, tag="dlin")
                tk.op("sp", lambda e, j=j, tsl0=tsl0, u_j=u_j: e.dma_start(
                    u_j.ap(), d_u.t[j * 128:(j + 1) * 128, tsl0:tsl0 + SEC]),
                    reads=[d_u], writes=[u_j], dma=True, tag="uin")
                tk.op("vector", lambda e, dl_j=dl_j, u_j=u_j, e_j=e_j: e.tensor_mul(
                    e_j.ap(), dl_j.ap(), u_j.ap()),
                    reads=[dl_j, u_j], writes=[e_j], tag="ej")
                for gl in range(16):
                    g = j * 16 + gl
                    da_g = da_db[gl % 2]
                    dbu_g = dbu_db[gl % 2]
                    h_g = h_db[gl % 2]
                    ch_g = ch_db[gl % 2]
                    for f in range(NSC):
                        psA = ctx.ps("A", 0, 512)
                        tk.op("tensor", lambda e, psA=psA, gl=gl, f=f, dl_j=dl_j: e.matmul(
                            psA.ap(), rrep.cview(gl * 128, 128).ap(), dl_j.cview(f * 512, 512).ap(),
                            start=True, stop=True),
                            reads=[rrep, dl_j], writes=[psA], tag="drep")
                        ac = acol.cview(g, 1)
                        tk.op("scalar", lambda e, psA=psA, f=f, ac=ac, da_g=da_g: e.activation(
                            da_g.cview(f * 512, 512).ap(), psA.ap(), AF.Exp, scale=ac.ap()),
                            reads=[psA, acol], writes=[da_g], tag="daexp")
                        psB = ctx.ps("B", 0, 512)
                        tk.op("tensor", lambda e, psB=psB, gl=gl, f=f, e_j=e_j: e.matmul(
                            psB.ap(), rrep.cview(gl * 128, 128).ap(), e_j.cview(f * 512, 512).ap(),
                            start=True, stop=True),
                            reads=[rrep, e_j], writes=[psB], tag="durep")
                        tk.op("vector", lambda e, psB=psB, f=f, tsl0=tsl0, dbu_g=dbu_g: e.tensor_mul(
                            dbu_g.cview(f * 512, 512).ap(), psB.ap(),
                            brep.cview(tsl0 + f * 512, 512).ap()),
                            reads=[psB, brep], writes=[dbu_g], tag="dbu")
                    cg = carry.cview(g, 1)
                    pass
                    if sec == 0:
                        tk.op("vector", lambda e, h_g=h_g, da_g=da_g, dbu_g=dbu_g: e.tensor_tensor_scan(
                            h_g.ap(), da_g.ap(), dbu_g.ap(), 0.0, ALU.mult, ALU.add),
                            reads=[da_g, dbu_g], writes=[h_g], tag="scan")
                    else:
                        tk.op("vector", lambda e, cg=cg, h_g=h_g, da_g=da_g, dbu_g=dbu_g: e.tensor_tensor_scan(
                            h_g.ap(), da_g.ap(), dbu_g.ap(), cg.ap(), ALU.mult, ALU.add),
                            reads=[da_g, dbu_g, carry], writes=[h_g], tag="scan")
                    if sec == 0:
                        tk.op("vector", lambda e, cg=cg, h_g=h_g: e.tensor_copy(cg.ap(), h_g.cview(SEC - 1, 1).ap()),
                              reads=[h_g], writes=[carry], tag="carrysave")
                    tk.op("vector", lambda e, tsl0=tsl0, ch_g=ch_g, h_g=h_g: e.tensor_mul(
                        ch_g.ap(), h_g.ap(), crep.cview(tsl0, SEC).ap()),
                        reads=[h_g, crep], writes=[ch_g], tag="chmul")
                    pend_ssum.append((g, ch_g, tsl0))
                    if len(pend_ssum) > 1:
                        flush_ssum(pend_ssum.pop(0))
            while pend_ssum:
                flush_ssum(pend_ssum.pop(0))
        A.release()

        if not mp(blk, 5):
            A.release()
            break
        A.mark()
        ys_j = A.alloc(T, BF16)
        uz_j = A.alloc(T, BF16)
        zz_j = A.alloc(T, BF16)
        sz_j = A.alloc(T, BF16)
        yo_j = A.alloc(T, BF16)
        for j in range(NJI):
            tk.op("sp", lambda e, j=j: e.dma_start(ys_j.ap(), d_ys.t[j * 128:(j + 1) * 128, :]),
                  reads=[d_ys], writes=[ys_j], dma=True, tag="ysin")
            tk.op("sp", lambda e, j=j: e.dma_start(uz_j.ap(), d_u.t[j * 128:(j + 1) * 128, :]),
                  reads=[d_u], writes=[uz_j], dma=True, tag="uin2")
            tk.op("sp", lambda e, j=j: e.dma_start(zz_j.ap(), d_z.t[j * 128:(j + 1) * 128, :]),
                  reads=[d_z], writes=[zz_j], dma=True, tag="zin")
            dcol = dpj.cview(j, 1)
            tk.op("vector", lambda e, dcol=dcol: e.scalar_tensor_tensor(
                yo_j.ap(), uz_j.ap(), dcol.ap(), ys_j.ap(), ALU.mult, ALU.add),
                reads=[uz_j, dpj, ys_j], writes=[yo_j], tag="uD")
            tk.op("scalar", lambda e: e.activation(sz_j.ap(), zz_j.ap(), AF.Silu),
                  reads=[zz_j], writes=[sz_j], tag="zsilu")
            tk.op("vector", lambda e: e.tensor_mul(yo_j.ap(), yo_j.ap(), sz_j.ap()),
                  reads=[yo_j, sz_j], writes=[yo_j], tag="ygate")
            tk.op("sp", lambda e, j=j: e.dma_start(d_yo.t[j * 128:(j + 1) * 128, :], yo_j.ap()),
                  reads=[yo_j], writes=[d_yo], dma=True, tag="yoout")
        A.release()

        if not mp(blk, 6):
            A.release()
            break
        A.mark()
        woT = A.alloc(NJI * 1024, BF16)
        tk.op("pool", lambda e, blk=blk: e.dma_start(
            woT.ap().rearrange("p (j d) -> p j d", j=NJI),
            W["mam_outT"].t[blk, :, :].rearrange("(j p) d -> p j d", p=128)),
            reads=[W["mam_outT"]], writes=[woT], dma=True, tag="woT")
        yosl_db = [A.alloc(NJI * 512, BF16) for _ in range(2)]
        ytsl_db = [A.alloc(512, BF16) for _ in range(2)]
        for f in range(8):
            yosl = yosl_db[f % 2]
            tk.op("sp", lambda e, yosl=yosl, f=f: e.dma_start(
                yosl.ap().rearrange("p (j c) -> p j c", j=NJI),
                d_yo.t[:, :].rearrange("(j p) t -> p j t", p=128)[:, :, f * 512:(f + 1) * 512]),
                reads=[d_yo], writes=[yosl], dma=True, tag="yosl")
            for op in range(NC):
                ps = ctx.ps("A" if op % 2 == 0 else "B", 0, 512)
                pairs = [(woT.cview(jc * 1024 + op * 128, 128), yosl.cview(jc * 512, 512))
                         for jc in range(NJI)]
                emit_mm_acc(ctx, ps, pairs, tag="outproj")
                if blk == 0:
                    dst = cur.cview(op * T + f * 512, 512)
                    tk.op("vector", lambda e, ps=ps, dst=dst: e.tensor_copy(dst.ap(), ps.ap()),
                          reads=[ps], writes=[dst], tag="curev")
                else:
                    ytsl = ytsl_db[op % 2]
                    tk.op("sp", lambda e, ytsl=ytsl, op=op, f=f: e.dma_start(
                        ytsl.ap(), d_yt.t[op * 128:(op + 1) * 128, f * 512:(f + 1) * 512]),
                        reads=[d_yt], writes=[ytsl], dma=True, tag="ytsl")
                    dst = cur.cview(op * T + f * 512, 512)
                    tk.op("vector", lambda e, ps=ps, ytsl=ytsl, dst=dst: e.tensor_add(
                        dst.ap(), ps.ap(), ytsl.ap()),
                        reads=[ps, ytsl], writes=[dst], tag="brev")
        A.release()
        A.release()

    xo_db = [A.alloc(1024, F32) for _ in range(2)]
    for i in range(NT):
        pt = ctx.ps("B" if i % 2 == 0 else "W", 0, 1024, BF16)
        for j in range(NC):
            src = cur.cview(j * T + i * 128, 128)
            dst = pt.cview(j * 128, 128)
            tk.op("tensor", lambda e, s=src, d=dst: e.transpose(d.ap(), s.ap(), ctx.ident.ap()),
                  reads=[src, ctx.ident], writes=[dst], tag="btr")
        xt = xt_db[i % 2]
        xo = xo_db[i % 2]
        row = i * 128
        tk.op("sp", lambda e, xt=xt, row=row: e.dma_start(
            xt.ap(), d_xin.t[row:row + 128, :]), reads=[d_xin], writes=[xt], dma=True, tag="xload2")
        tk.op("vector", lambda e, pt=pt, xt=xt, xo=xo: e.tensor_add(
            xo.ap(), pt.ap(), xt.ap()),
            reads=[pt, xt], writes=[xo], tag="resadd")
        tk.op("sp", lambda e, xo=xo, row=row: e.dma_start(
            d_xout.t[row:row + 128, :], xo.ap()), reads=[xo], writes=[d_xout], dma=True, tag="xstore")
    A.release()



def _prep_host(inputs):
    f32 = np.float32
    H = {}
    H["x_full"] = np.ascontiguousarray(inputs["L0_attn_x"], dtype=f32)
    H["l0f_wrT"] = np.ascontiguousarray(np.asarray(inputs["L0_ffn_W_r"], f32).T)
    H["l0f_wkT"] = np.ascontiguousarray(np.asarray(inputs["L0_ffn_W_k"], f32).T)
    H["l0f_wvT"] = np.ascontiguousarray(np.asarray(inputs["L0_ffn_W_v"], f32).T)
    H["l0f_mur"] = np.ascontiguousarray(np.asarray(inputs["L0_ffn_mu_r"], f32).reshape(8, 128).T)
    H["l0f_muk"] = np.ascontiguousarray(np.asarray(inputs["L0_ffn_mu_k"], f32).reshape(8, 128).T)
    H["moe_w1"] = np.ascontiguousarray(np.asarray(inputs["L1_ffn_W1"], f32))
    H["moe_b1pj"] = np.ascontiguousarray(
        np.asarray(inputs["L1_ffn_b1"], f32).reshape(8, 32, 128).transpose(0, 2, 1))
    H["moe_w2"] = np.ascontiguousarray(np.asarray(inputs["L1_ffn_W2"], f32))
    H["moe_b2"] = np.ascontiguousarray(np.asarray(inputs["L1_ffn_b2"], f32))
    H["b_wqT"] = np.ascontiguousarray(np.asarray(inputs["L0_attn_Wq"], f32).transpose(0, 2, 1))
    H["b_wkT"] = np.ascontiguousarray(np.asarray(inputs["L0_attn_Wk"], f32).transpose(0, 2, 1))
    H["b_bqpj"] = np.ascontiguousarray(np.asarray(inputs["L0_attn_bq"], f32).reshape(6, 8, 128).transpose(0, 2, 1))
    H["b_bkpj"] = np.ascontiguousarray(np.asarray(inputs["L0_attn_bk"], f32).reshape(6, 8, 128).transpose(0, 2, 1))
    for nm, key in (("b_ln1w", "L0_attn_ln1_w"), ("b_ln1b", "L0_attn_ln1_b"),
                    ("b_ln2w", "L0_attn_ln2_w"), ("b_ln2b", "L0_attn_ln2_b"),
                    ("b_ln3w", "L0_attn_ln3_w"), ("b_ln3b", "L0_attn_ln3_b"),
                    ("b_b2", "L0_attn_b2")):
        H[nm] = np.ascontiguousarray(np.asarray(inputs[key], f32))
    H["b_w1T"] = np.ascontiguousarray(np.asarray(inputs["L0_attn_W1"], f32).transpose(0, 2, 1))
    H["b_b1pj"] = np.ascontiguousarray(np.asarray(inputs["L0_attn_b1"], f32).reshape(6, 32, 128).transpose(0, 2, 1))
    H["b_w2T"] = np.ascontiguousarray(np.asarray(inputs["L0_attn_W2"], f32).transpose(0, 2, 1))
    pj = np.arange(8064)[None, :] - np.arange(128)[:, None]
    H["b_relt"] = np.ascontiguousarray(np.exp(-np.abs(3968.0 - pj) / 16.0).astype(f32))
    inw = np.asarray(inputs["L1_attn_in_proj_w"], f32)
    xpw = np.asarray(inputs["L1_attn_x_proj_w"], f32)
    dtw = np.asarray(inputs["L1_attn_dt_proj_w"], f32)
    onw = np.asarray(inputs["L1_attn_out_proj_w"], f32)
    cw = np.asarray(inputs["L1_attn_conv_w"], f32)[:, :, 0, :]
    cb = np.asarray(inputs["L1_attn_conv_b"], f32)
    dtb = np.asarray(inputs["L1_attn_dt_proj_b"], f32)
    alog = np.asarray(inputs["L1_attn_A_log"], f32)
    Dp = np.asarray(inputs["L1_attn_D"], f32)
    H["mam_inT"] = np.ascontiguousarray(inw.transpose(0, 2, 1))
    x1T = xpw[:, 0:64].transpose(0, 2, 1)
    x2T = xpw[:, 64:96].transpose(0, 2, 1)
    H["mam_x1T"] = np.ascontiguousarray(x1T.reshape(2, 16, 128, 64).transpose(0, 2, 1, 3).reshape(2, 128, 1024))
    H["mam_x2T"] = np.ascontiguousarray(x2T.reshape(2, 16, 128, 32).transpose(0, 2, 1, 3).reshape(2, 128, 512))
    H["mam_dtT"] = np.ascontiguousarray(dtw.transpose(0, 2, 1))
    H["mam_outT"] = np.ascontiguousarray(onw.transpose(0, 2, 1))
    H["mam_apj"] = np.ascontiguousarray(alog.reshape(2, 32768).reshape(2, 256, 128).transpose(0, 2, 1))
    H["mam_dpj"] = np.ascontiguousarray(Dp.reshape(2, 16, 128).transpose(0, 2, 1))
    H["mam_cwpj"] = np.ascontiguousarray(cw.reshape(2, 16, 128, 4).transpose(0, 2, 1, 3).reshape(2, 128, 64))
    H["mam_cbpj"] = np.ascontiguousarray(cb.reshape(2, 16, 128).transpose(0, 2, 1))
    H["mam_dtbpj"] = np.ascontiguousarray(dtb.reshape(2, 16, 128).transpose(0, 2, 1))
    cc = np.arange(128)
    pp = np.arange(128)
    H["mam_rrep"] = np.ascontiguousarray(np.stack(
        [(cc[:, None] == (8 * g + pp[None, :] // 16)).astype(f32) for g in range(16)]))
    p64 = np.arange(64)
    H["mam_s8"] = np.ascontiguousarray(np.stack(
        [(p64[None, :] == (8 * q + cc[:, None] // 16)).astype(f32) for q in range(8)]))
    c32 = np.arange(32)
    H["mam_r16b"] = np.ascontiguousarray((c32[:, None] == (pp[None, :] % 16)).astype(f32))
    H["mam_r16c"] = np.ascontiguousarray((c32[:, None] == (16 + pp[None, :] % 16)).astype(f32))
    return H


STAGE_DRAMS = {
    "ffn0": {
        "l0f_wrT": ((C, C), F32, "ExternalInput"),
        "l0f_wkT": ((C, C), F32, "ExternalInput"),
        "l0f_wvT": ((C, C), F32, "ExternalInput"),
        "l0f_mur": ((128, 8), F32, "ExternalInput"),
        "l0f_muk": ((128, 8), F32, "ExternalInput"),
    },
    "attn": {
        "b_wqT": ((6, C, C), F32, "ExternalInput"),
        "b_wkT": ((6, C, C), F32, "ExternalInput"),
        "b_bqpj": ((6, 128, 8), F32, "ExternalInput"),
        "b_bkpj": ((6, 128, 8), F32, "ExternalInput"),
        "b_ln1w": ((6, C), F32, "ExternalInput"),
        "b_ln1b": ((6, C), F32, "ExternalInput"),
        "b_ln2w": ((6, C), F32, "ExternalInput"),
        "b_ln2b": ((6, C), F32, "ExternalInput"),
        "b_ln3w": ((6, C), F32, "ExternalInput"),
        "b_ln3b": ((6, C), F32, "ExternalInput"),
        "b_w1T": ((6, C, 4096), F32, "ExternalInput"),
        "b_b1pj": ((6, 128, 32), F32, "ExternalInput"),
        "b_w2T": ((6, 4096, C), F32, "ExternalInput"),
        "b_b2": ((6, C), F32, "ExternalInput"),
        "b_relt": ((128, 8064), F32, "ExternalInput"),
        "b_h": ((T, C), F32, "Internal"),
        "b_h2": ((T, C), F32, "Internal"),
        "b_qt": ((C, T), BF16, "Internal"),
        "b_pt": ((32, 128, T), BF16, "Internal"),
    },
    "mamba": {
        "mam_inT": ((2, C, 4096), F32, "ExternalInput"),
        "mam_x1T": ((2, 128, 1024), F32, "ExternalInput"),
        "mam_x2T": ((2, 128, 512), F32, "ExternalInput"),
        "mam_dtT": ((2, 64, DI), F32, "ExternalInput"),
        "mam_outT": ((2, DI, C), F32, "ExternalInput"),
        "mam_apj": ((2, 128, 256), F32, "ExternalInput"),
        "mam_dpj": ((2, 128, 16), F32, "ExternalInput"),
        "mam_cwpj": ((2, 128, 64), F32, "ExternalInput"),
        "mam_cbpj": ((2, 128, 16), F32, "ExternalInput"),
        "mam_dtbpj": ((2, 128, 16), F32, "ExternalInput"),
        "mam_rrep": ((16, 128, 128), F32, "ExternalInput"),
        "mam_s8": ((8, 128, 64), F32, "ExternalInput"),
        "mam_r16b": ((32, 128), F32, "ExternalInput"),
        "mam_r16c": ((32, 128), F32, "ExternalInput"),
        "mam_yt": ((C, T), BF16, "Internal"),
        "mam_u": ((DI, T), BF16, "Internal"),
        "mam_z": ((DI, T), BF16, "Internal"),
        "mam_dl": ((DI, T), BF16, "Internal"),
        "mam_ys": ((DI, T), BF16, "Internal"),
        "mam_yo": ((DI, T), BF16, "Internal"),
    },
    "moe": {
        "moe_w1": ((8, C, 4096), F32, "ExternalInput"),
        "moe_b1pj": ((8, 128, 32), F32, "ExternalInput"),
        "moe_w2": ((8, 4096, C), F32, "ExternalInput"),
        "moe_b2": ((8, C), F32, "ExternalInput"),
        "moe_yt": ((128, NC * T), BF16, "Internal"),
        "moe_wt": ((256, 128), BF16, "Internal"),
        "moe_acc": ((128, NC * T), F32, "Internal"),
    },
}

STAGE_KIND = {"A": "ffn0", "B": "attn", "C": "ffn0", "D": "moe", "E": "mamba", "F": "moe"}


def _emit_stage(ctx, stage, d, d_in, d_out):
    kind = STAGE_KIND[stage]
    if kind == "ffn0":
        Wf = {"wrT": d["l0f_wrT"], "wkT": d["l0f_wkT"], "wvT": d["l0f_wvT"],
              "mur_pj": d["l0f_mur"], "muk_pj": d["l0f_muk"]}
        emit_stage_ffn0(ctx, d_in, d_out, Wf, alpha=0.5)
    elif kind == "moe":
        Wm = {"moe_w1": d["moe_w1"], "moe_b1pj": d["moe_b1pj"], "moe_w2": d["moe_w2"],
              "moe_b2": d["moe_b2"], "d_yt": d["moe_yt"], "d_wt": d["moe_wt"],
              "d_acc": d["moe_acc"]}
        emit_stage_moe(ctx, d_in, d_out, Wm, alpha=0.5)
    elif kind == "attn":
        Wa = {k: d[k] for k in STAGE_DRAMS["attn"]}
        emit_stage_attn(ctx, d_in, d_out, Wa)
    elif kind == "mamba":
        Wb = {k: d[k] for k in STAGE_DRAMS["mamba"]}
        emit_stage_mamba(ctx, d_in, d_out, Wb)
    else:
        raise ValueError(stage)


def _build_chain(stages):
    nc = bass.Bass("TRN2", target_bir_lowering=False)
    d = {}

    def dram(name, shape, dtype, kind):
        if name in d:
            return d[name]
        t = nc.dram_tensor(name, list(shape), dtype, kind=kind)
        d[name] = DramT(t, "d_" + name, shape, dtype)
        return d[name]

    dram("x", (T, C), F32, "ExternalInput")
    dram("y", (T, C), F32, "ExternalOutput")
    kinds = {STAGE_KIND[s] for s in stages}
    for k in kinds:
        for name, (shape, dtype, dkind) in STAGE_DRAMS[k].items():
            dram(name, shape, dtype, dkind)
    if len(stages) > 1:
        dram("xp0", (T, C), F32, "Internal")
        dram("xp1", (T, C), F32, "Internal")

    tk = TK(nc)
    arena = Arena(nc, "arena", 206 * 1024)
    ctx = Ctx(nc, tk, arena)
    cur = d["x"]
    for si, s in enumerate(stages):
        last = si == len(stages) - 1
        nxt = d["y"] if last else d["xp%d" % (si % 2)]
        _emit_stage(ctx, s, d, cur, nxt)
        cur = nxt
    tk.barrier_read_all_dmas()
    tk.finalize()
    nc.finalize()
    return nc


CHAIN = ["A", "B", "C", "D", "E", "F"]
GROUPS = [["A", "B", "C", "D", "E", "F"]]


def kernel(**inputs):
    H = _prep_host(inputs)
    x = H["x_full"]
    groups = GROUPS if GROUPS is not None else [CHAIN]
    wkeys = set()
    for g in groups:
        for s in g:
            for name, (_, _, k) in STAGE_DRAMS[STAGE_KIND[s]].items():
                if k == "ExternalInput":
                    wkeys.add(name)
    cur = [np.ascontiguousarray(x[c]) for c in range(8)]
    for g in groups:
        nc = _build_chain(g)
        gw = set()
        for s in g:
            for name, (_, _, k) in STAGE_DRAMS[STAGE_KIND[s]].items():
                if k == "ExternalInput":
                    gw.add(name)
        in_maps = []
        for c in range(8):
            m = {"x": cur[c]}
            for name in gw:
                m[name] = H[name]
            in_maps.append(m)
        res = run_bass_kernel_spmd(nc, in_maps, core_ids=list(range(8)))
        cur = [np.ascontiguousarray(r["y"]) for r in res.results]
    return np.stack(cur, axis=0)

